# version 13
# baseline (speedup 1.0000x reference)
import numpy as np
import ml_dtypes
from contextlib import ExitStack
import concourse.bass as bass
import concourse.mybir as mybir
from concourse.bass_utils import run_bass_kernel_spmd
import concourse.bass_utils as _bu

if not getattr(_bu, "_ldw_opt_patched", False):
    _orig_run_command = _bu.run_command

    def _run_command_ldw(argv, **kw):
        argv = ["--enable-ldw-opt=true" if a == "--enable-ldw-opt=false"
                else a for a in argv]
        return _orig_run_command(argv, **kw)

    _bu.run_command = _run_command_ldw
    _bu._ldw_opt_patched = True

F32 = mybir.dt.float32
BF16 = mybir.dt.bfloat16
AF = mybir.ActivationFunctionType
ALU = mybir.AluOpType
AX = mybir.AxisListType.X

D = 1024
DS = 16
SEQ = 2048
NCH = 32
CPS = 4
NSC = NCH // CPS
DFF = 4096
NFT = 32
NDT = 8
EPS1 = 1e-5
EPS2 = 1e-8


class Ctr:
    def __init__(self, sem):
        self.sem = sem
        self.n = 0


RACE_DEBUG = False
DETECT = False
STAGE = 3


def build_nc():
    nc = bass.Bass("TRN2", target_bir_lowering=False,
                   detect_race_conditions=RACE_DEBUG or DETECT)

    x_d = nc.dram_tensor("x", [2, SEQ, D], F32, kind="ExternalInput")
    wup_d = nc.dram_tensor("wup", [DFF, D], F32, kind="ExternalInput")
    wdn_d = nc.dram_tensor("wdn", [D, DFF], F32, kind="ExternalInput")
    alog_d = nc.dram_tensor("alog", [D, DS], F32, kind="ExternalInput")
    bw_d = nc.dram_tensor("bw", [DS, D], F32, kind="ExternalInput")
    cw_d = nc.dram_tensor("cw", [D, DS], F32, kind="ExternalInput")
    dv_d = nc.dram_tensor("dv", [1, D], F32, kind="ExternalInput")
    ga_d = nc.dram_tensor("ga", [1, D], F32, kind="ExternalInput")
    be_d = nc.dram_tensor("be", [1, D], F32, kind="ExternalInput")
    idb_d = nc.dram_tensor("idb", [128, 128], BF16, kind="ExternalInput")
    idf_d = nc.dram_tensor("idf", [128, 128], F32, kind="ExternalInput")
    onesb_d = nc.dram_tensor("onesb", [128, 1], BF16, kind="ExternalInput")
    onesf_d = nc.dram_tensor("onesf", [128, 128], F32, kind="ExternalInput")
    nkrev_d = nc.dram_tensor("nkrev", [128, 1], F32, kind="ExternalInput")
    y_d = nc.dram_tensor("y", [2, SEQ, D], F32, kind="ExternalOutput")
    erow_d = nc.dram_tensor("erow_scr", [1, DS * D], F32, kind="Internal")
    gp_d = nc.dram_tensor("gp_scr", [DS, 384], BF16, kind="Internal")
    wup_scr = nc.dram_tensor("wup_scr", [NFT // 2, 128, 2, NDT, 128], BF16,
                             kind="Internal")

    sb = nc.alloc_sbuf_tensor
    wupb = sb("wupb", [128, 2, 2, NDT, 128], BF16)
    WdnT = sb("WdnT", [128, NFT, D], BF16)
    T_all = sb("T_all", [128, DS, 256], BF16)
    bxs = sb("bxs", [128, 34, DS], BF16)
    bT = sb("bT", [128, NDT, DS], BF16)
    cT = sb("cT", [16, NDT * 128], BF16)
    Dbc = sb("Dbc", [128, D], F32)
    gbc = sb("gbc", [128, D], BF16)
    bbc = sb("bbc", [128, D], BF16)
    idb = sb("idb_s", [128, 128], BF16)
    onesb = sb("onesb_s", [128, 1], BF16)
    nkrev = sb("nkrev_s", [128, 1], F32)
    onef = sb("onef", [1, 1], F32)
    tau_u = sb("tau_u", [128, 1], F32)
    tau_d = sb("tau_d", [128, 1], F32)
    acw = sb("acw", [128, 34], F32)


    pa = nc.alloc_psum_tensor
    trA = pa("trA", [128, 1024], BF16)
    misc1 = pa("misc1", [128, 512], F32)

    s_sm = Ctr(nc.alloc_semaphore("s_sm"))
    s_on = Ctr(nc.alloc_semaphore("s_on"))
    sw = Ctr(nc.alloc_semaphore("sw"))
    sp1 = [Ctr(nc.alloc_semaphore("sp1a")), Ctr(nc.alloc_semaphore("sp1b"))]
    sp2 = [Ctr(nc.alloc_semaphore("sp2a")), Ctr(nc.alloc_semaphore("sp2b")),
           Ctr(nc.alloc_semaphore("sp2c"))]
    sxc = [Ctr(nc.alloc_semaphore(f"sxc{i}")) for i in range(7)]
    soo = [Ctr(nc.alloc_semaphore("soa")), Ctr(nc.alloc_semaphore("sob"))]
    swp = [Ctr(nc.alloc_semaphore("swpa")), Ctr(nc.alloc_semaphore("swpb")),
           Ctr(nc.alloc_semaphore("swpc")), Ctr(nc.alloc_semaphore("swpd"))]
    sgE = [Ctr(nc.alloc_semaphore("sgE0")), Ctr(nc.alloc_semaphore("sgE1"))]
    sv = Ctr(nc.alloc_semaphore("sv"))
    sa = Ctr(nc.alloc_semaphore("sa"))
    stt = Ctr(nc.alloc_semaphore("stt"))
    sg = Ctr(nc.alloc_semaphore("sg"))

    ops = {k: [] for k in ("sync", "scalar", "vector", "tensor", "gpsimd")}

    def em(eng, fn):
        ops[eng].append(fn)

    def dma(eng, ctr, out_ap, in_ap):
        def f(e, o=out_ap, i=in_ap, s=ctr.sem):
            e.dma_start(out=o, in_=i).then_inc(s, 16)
        em(eng, f)
        ctr.n += 16
        return ctr.n

    def wait(eng, ctr, v):
        if v <= 0:
            return
        s = ctr.sem if isinstance(ctr, Ctr) else ctr

        def f(e, s=s, v=v):
            e.wait_ge(s, v)
        em(eng, f)

    eng_ctr = {"vector": sv, "scalar": sa, "tensor": stt}

    def run(eng, fn, inc=None):
        if RACE_DEBUG and eng in eng_ctr:
            c = eng_ctr[eng]
            if c.n:
                wait(eng, c, c.n)
            inc = c

        if inc is None:
            em(eng, lambda e, fn=fn: fn(e))
            return None

        def f(e, fn=fn, s=inc.sem):
            fn(e).then_inc(s, 1)
        em(eng, f)
        inc.n += 1
        return inc.n

    x_at = sb("x_at", [128, 6, D], F32)
    xnbf = sb("xnbf", [128, CPS, D], BF16)
    stc = sb("stc", [128, 24 * CPS], F32)
    scol = sb("scol", [128, CPS], F32)
    invD = sb("invD", [128, 1], F32)
    invDFF = sb("invDFF", [128, 1], F32)
    (MUR, QQ, VARP, VEPS, T1, SVC, RR, AC, T0, C1, NB,
     M2R, Q2, V2P, V2E, T2, R2, N2) = range(18)

    o2_rel = [0, 0]
    misc1_rel = (None, 0)
    mm1_rel = [0, 0]
    prev_mm1_done = 0
    prev_mm2_done = 0
    prev_asm = {}
    prev_xn = {}
    prev_xq = {}
    prev_dx = {}
    m_l2m = {}
    tr_mark = {}
    junk_rel = [0]
    prev_q_last = 0
    m_out = {}
    m_xld = {}
    last_asm_mark = 0
    wup_ld = {}
    wup_rel = [0, 0]
    cst = [0]
    XSLOTS = 6

    def col(k, cc):
        c = cc % CPS
        return stc[:, k + 24 * c:k + 24 * c + 1]

    def emit_x_loads(sc, chunks=None):
        for c in (range(CPS) if chunks is None else chunks):
            cc = sc * CPS + c
            r, jj = cc // 16, cc % 16
            xslot = cc % XSLOTS
            if cc >= XSLOTS:
                wait("sync", sa, prev_xq.get(cc - XSLOTS, 0))
                wait("sync", sv, prev_asm.get(cc - XSLOTS, 0))
            xsrc = bass.AP(x_d, (r * SEQ + jj * 128) * D, [[D, 128], [1, D]])
            m_xld[cc] = dma("sync", sxc[xslot], x_at[:, xslot, :], xsrc)

    def attn_pre_gen(sc, chunks=None):
        chunks = list(range(CPS) if chunks is None else chunks)
        ccs = [sc * CPS + c for c in chunks]
        v1 = {}
        mq = {}
        vc = {}
        for c, cc in zip(chunks, ccs):
            xslot = cc % XSLOTS
            wait("vector", sxc[xslot], m_xld[cc])
            v1[cc] = run("vector", lambda e, xslot=xslot, cc=cc:
                         e.tensor_reduce(
                col(MUR, cc), x_at[:, xslot, :], AX, ALU.add), sv)
        yield
        for c, cc in zip(chunks, ccs):
            xslot = cc % XSLOTS
            wait("scalar", sxc[xslot], m_xld[cc])
            wait("scalar", sa, junk_rel[0])
            if cc >= CPS:
                wait("scalar", stt, tr_mark.get(cc - CPS, 0))
            m_q = run("scalar", lambda e, xslot=xslot, cc=cc, c=c:
                      e.activation(
                xnbf[:, c, :], x_at[:, xslot, :], AF.Square,
                accum_out=col(QQ, cc)), sa)
            junk_rel[0] = m_q
            prev_xq[cc] = m_q
            mq[cc] = m_q
        yield
        for c, cc in zip(chunks, ccs):
            wait("vector", sa, mq[cc])
            wait("vector", sv, v1[cc])
            v2a = run("vector", lambda e, cc=cc: e.scalar_tensor_tensor(
                col(T0, cc), col(MUR, cc), 1.0 / D, col(MUR, cc),
                ALU.mult, ALU.mult), sv)
            wait("vector", sv, v2a)
            v2b = run("vector", lambda e, cc=cc: e.tensor_sub(
                col(VARP, cc), col(QQ, cc), col(T0, cc)), sv)
            wait("vector", sv, v2b)
            vc[cc] = run("vector", lambda e, cc=cc: e.tensor_scalar(
                col(VEPS, cc), col(VARP, cc), 1.0 / D, EPS1,
                ALU.mult, ALU.add), sv)
        yield
        s2m = {}
        for c, cc in zip(chunks, ccs):
            wait("scalar", sv, vc[cc])
            if cc < CPS:
                wait("scalar", sv, cst[0])
            run("scalar", lambda e, cc=cc: e.activation(
                col(T1, cc), col(VEPS, cc), AF.Sqrt), sa)
            s2m[cc] = run("scalar", lambda e, cc=cc: e.activation(
                col(SVC, cc), col(VARP, cc), AF.Sqrt, scale=invD[:]), sa)
        yield
        v7 = {}
        for c, cc in zip(chunks, ccs):
            wait("vector", sa, s2m[cc])
            v3 = run("vector", lambda e, cc=cc: e.reciprocal(
                col(RR, cc), col(T1, cc)), sv)
            wait("vector", sv, v3)
            v4 = run("vector", lambda e, cc=cc: e.tensor_scalar(
                col(AC, cc), col(SVC, cc), col(RR, cc), EPS2,
                ALU.mult, ALU.add), sv)
            wait("vector", sv, v4)
            v5 = run("vector", lambda e, cc=cc: e.reciprocal(
                col(T0, cc), col(AC, cc)), sv)
            wait("vector", sv, v5)
            v6 = run("vector", lambda e, cc=cc: e.tensor_mul(
                col(C1, cc), col(RR, cc), col(T0, cc)), sv)
            wait("vector", sv, v6)
            v7[cc] = run("vector", lambda e, cc=cc: e.scalar_tensor_tensor(
                col(NB, cc), col(MUR, cc), -1.0 / D, col(C1, cc),
                ALU.mult, ALU.mult), sv)
        yield
        for c, cc in zip(chunks, ccs):
            xslot = cc % XSLOTS
            wait("vector", sv, v7[cc])
            wait("vector", sa, junk_rel[0])
            if cc >= CPS:
                wait("vector", stt, tr_mark.get(cc - CPS, 0))
            m_xn = run("vector", lambda e, xslot=xslot, cc=cc, c=c:
                       e.tensor_scalar(
                xnbf[:, c, :], x_at[:, xslot, :], col(C1, cc), col(NB, cc),
                ALU.mult, ALU.add), sv)
            prev_xn[cc] = m_xn
        yield

    def ln2_gen(sc, args):
        ccs = [sc * CPS + c for c in range(CPS)]
        w1 = {}
        q2 = {}
        tc = {}
        sq2 = {}
        for c, cc in zip(range(CPS), ccs):
            m_dxa, yfb = args[c]
            wait("vector", sv, m_dxa)
            w1[cc] = run("vector", lambda e, cc=cc, yfb=yfb: e.tensor_reduce(
                col(M2R, cc), yfb, AX, ALU.add), sv)
        yield
        for c, cc in zip(range(CPS), ccs):
            m_dxa, yfb = args[c]
            wait("scalar", sv, m_dxa)
            wait("scalar", sa, junk_rel[0])
            wait("scalar", stt, tr_mark[cc])
            q2[cc] = run("scalar", lambda e, cc=cc, c=c, yfb=yfb:
                         e.activation(
                xnbf[:, c, :], yfb, AF.Square, accum_out=col(Q2, cc)), sa)
            junk_rel[0] = q2[cc]
        yield
        for c, cc in zip(range(CPS), ccs):
            wait("vector", sa, q2[cc])
            wait("vector", sv, w1[cc])
            t2a = run("vector", lambda e, cc=cc: e.scalar_tensor_tensor(
                col(T2, cc), col(M2R, cc), 1.0 / D, col(M2R, cc),
                ALU.mult, ALU.mult), sv)
            wait("vector", sv, t2a)
            t2b = run("vector", lambda e, cc=cc: e.tensor_sub(
                col(V2P, cc), col(Q2, cc), col(T2, cc)), sv)
            wait("vector", sv, t2b)
            tc[cc] = run("vector", lambda e, cc=cc: e.tensor_scalar(
                col(V2E, cc), col(V2P, cc), 1.0 / D, EPS1,
                ALU.mult, ALU.add), sv)
        yield
        for c, cc in zip(range(CPS), ccs):
            wait("scalar", sv, tc[cc])
            sq2[cc] = run("scalar", lambda e, cc=cc: e.activation(
                col(T2, cc), col(V2E, cc), AF.Sqrt), sa)
        yield
        for c, cc in zip(range(CPS), ccs):
            m_dxa, yfb = args[c]
            wait("vector", sa, sq2[cc])
            r2m = run("vector", lambda e, cc=cc: e.reciprocal(
                col(R2, cc), col(T2, cc)), sv)
            wait("vector", sv, r2m)
            n2m = run("vector", lambda e, cc=cc: e.scalar_tensor_tensor(
                col(N2, cc), col(M2R, cc), -1.0 / D, col(R2, cc),
                ALU.mult, ALU.mult), sv)
            wait("vector", sv, n2m)
            m_l2 = run("vector", lambda e, c=c, cc=cc, yfb=yfb:
                       e.tensor_scalar(
                ln2_sb[:, c, :], yfb, col(R2, cc), col(N2, cc),
                ALU.mult, ALU.add), sv)
            m_l2m[cc] = m_l2
        yield

    stepq = []

    def flush_steps(n):
        done = 0
        while stepq and done < n:
            try:
                next(stepq[0])
                done += 1
            except StopIteration:
                stepq.pop(0)

    def flush_all():
        while stepq:
            flush_steps(1000)

    with ExitStack() as pro:
        psb = lambda name, shape, dt: pro.enter_context(
            nc.sbuf_tensor(name, shape, dt))
        onesf = psb("onesf_s", [128, 128], F32)
        idf = psb("idf_s", [128, 128], F32)
        Asb = psb("Asb", [128, 128], F32)
        Esb = psb("Esb", [128, 128], F32)
        EsbT = psb("EsbT", [128, 128], F32)
        Ebc = psb("Ebc", [128, 2, 1024], F32)
        Msb = psb("Msb", [128, 1024], F32)
        g_sb = psb("g_sb", [128, DS], F32)
        gbf = psb("gbf", [128, DS], BF16)
        gT = psb("gT", [16, 128], BF16)
        zrow = psb("zrow", [16, 128], BF16)
        hank = psb("hank", [128, DS, 256], BF16)
        Bbf = psb("Bbf", [16, D], BF16)
        Csb = psb("Csb", [128, NDT, DS], F32)
        Cbf = psb("Cbf", [128, NDT, DS], BF16)
        p1a = psb("p1a", [128, 2048], F32)
        p1b = psb("p1b", [128, 2048], F32)
        p2a = psb("p2a", [128, 2048], F32)
        p2b = psb("p2b", [128, 2048], F32)
        p2c = psb("p2c", [128, 2048], F32)
        t1bf = psb("t1bf", [128, 2048], BF16)
        wtbf = psb("wtbf", [128, 4096], BF16)
        wstg = psb("wstg", [128, 2, 1024], BF16)
        trB = pro.enter_context(nc.psum_tensor("trB", [128, 1024], BF16))
        trC = pro.enter_context(nc.psum_tensor("trC", [128, 1024], BF16))
        trD = pro.enter_context(nc.psum_tensor("trD", [128, 1024], BF16))

        Asb3 = Asb[:].rearrange("p (a b) -> p a b", a=NDT)
        for hq in range(2):
            asrc = bass.AP(alog_d, hq * 4 * 128 * DS,
                           [[DS, 128], [128 * DS, 4], [1, DS]])
            dma("sync", s_sm, Asb3[:, hq * 4:(hq + 1) * 4, :], asrc)
        for hq in range(2):
            csrc = bass.AP(cw_d, hq * 4 * 128 * DS,
                           [[DS, 128], [128 * DS, 4], [1, DS]])
            dma("sync", s_sm, Csb[:, hq * 4:(hq + 1) * 4, :], csrc)
        dma("sync", s_sm, Dbc[:], bass.AP(dv_d, 0, [[0, 128], [1, D]]))
        dma("sync", s_sm, idb[:], idb_d.ap())
        dma("sync", s_sm, idf[:], idf_d.ap())
        dma("sync", s_sm, onesb[:], onesb_d.ap())
        dma("sync", s_sm, nkrev[:], nkrev_d.ap())
        SM = s_sm.n
        dma("sync", s_on, onesf[:], onesf_d.ap())
        dma("gpsimd", sg, gbc[:], bass.AP(ga_d, 0, [[0, 128], [1, D]]))
        dma("gpsimd", sg, Bbf[:], bw_d.ap())
        m_be = dma("gpsimd", sg, bbc[:], bass.AP(be_d, 0, [[0, 128], [1, D]]))

        emit_x_loads(0)
        stepq.append(attn_pre_gen(0))

        run("vector", lambda e: e.memset(invD[:], 1.0 / D), sv)
        run("vector", lambda e: e.memset(onef[:], 1.0), sv)
        cst[0] = run("vector", lambda e: e.memset(invDFF[:], 1.0 / DFF), sv)
        wait("vector", s_sm, SM)
        m_cbf = run("vector", lambda e: e.tensor_copy(Cbf[:], Csb[:]), sv)
        run("vector", lambda e: e.memset(bxs[:], 0.0), sv)

        wait("scalar", s_sm, SM)
        m_exp = run("scalar", lambda e: e.activation(Esb[:], Asb[:], AF.Exp), sa)
        wait("tensor", s_sm, SM)
        wait("tensor", sa, m_exp)
        m_etr = run("tensor",
                    lambda e: e.transpose(misc1[:, 0:128], Esb[:], idf[:]), stt)
        wait("vector", stt, m_etr)
        m_etre = run("vector", lambda e: e.tensor_copy(EsbT[:], misc1[:, 0:128]),
                     sv)
        wait("scalar", sv, m_etre)
        for dt in range(NDT):
            edst = bass.AP(erow_d, dt * 128, [[D, DS], [1, 128]])
            m_er = dma("scalar", sw, edst, EsbT[dt * 16:(dt + 1) * 16, :])

        wait("tensor", sv, m_cbf)
        for dt in range(NDT):
            m_bctr = run("tensor", lambda e, dt=dt: e.transpose(
                trA[0:16, dt * 128:(dt + 1) * 128], Cbf[:, dt, :], idb[:]), stt)
        wait("vector", stt, m_bctr)
        m_cT = run("vector", lambda e: e.tensor_copy(cT[:], trA[0:16, :]), sv)
        wait("tensor", sv, m_cT)
        wait("tensor", sg, m_be)
        for dt in range(NDT):
            m_btr = run("tensor", lambda e, dt=dt: e.transpose(
                trA[0:128, dt * 16:(dt + 1) * 16],
                Bbf[:, dt * 128:(dt + 1) * 128], idb[0:16, 0:16]), stt)
        wait("vector", stt, m_btr)
        m_bT0 = run("vector", lambda e: e.tensor_copy(
            bT[:].rearrange("p a b -> p (a b)"), trA[0:128, 0:128]), sv)

        pool = [p1a, p1b, p2a, p2b, p2c]
        psem = [sp1[0], sp1[1], sp2[0], sp2[1], sp2[2]]
        buf_rel = {}

        def pool_load(b, src, rearr):
            if b in buf_rel:
                wait("sync", *buf_rel[b])
            dst = (pool[b][:].rearrange("p (a b) -> p a b", a=2) if rearr
                   else pool[b][:])
            return dma("sync", psem[b], dst, src)

        m_absA = []
        for i in range(16):
            src = bass.AP(wup_d, i * 2 * 128 * D,
                          [[D, 128], [128 * D, 2], [1, D]])
            b = i % 5
            mld = pool_load(b, src, True)
            wait("vector", psem[b], mld)
            m_ab = run("vector", lambda e, buf=pool[b], i=i: e.tensor_reduce(
                acw[:, i:i + 1], buf[:], AX, ALU.add,
                apply_absolute_value=True), sv)
            m_absA.append(m_ab)
            buf_rel[b] = (sv, m_ab)
            flush_steps(1)

        wait("vector", sv, m_absA[15])
        m_ru = run("vector", lambda e: e.tensor_reduce(
            acw[:, 32:33], acw[:, 0:16], AX, ALU.add), sv)
        wait("tensor", s_on, 16)
        wait("tensor", sv, m_ru)
        m_ammu = run("tensor", lambda e: e.matmul(
            misc1[:, 0:1], onesf[:], acw[:, 32:33], start=True, stop=True), stt)
        wait("vector", stt, m_ammu)
        m_tau_u = run("vector", lambda e: e.tensor_scalar(
            tau_u[:], misc1[:, 0:1], 0.5 / (DFF * D), None, ALU.mult), sv)

        banks = [trA, trB, trC, trD]
        bank_rel = [0, 0, 0, 0]
        reg_rel = [0, 0]
        upar = [0]
        stg_dma = [0, 0, 0, 0]
        hankf = hank[:].rearrange("p a b -> p (a b)")
        stg_slot = [wstg[:, 0, :], wstg[:, 1, :],
                    hankf[:, 0:1024], hankf[:, 1024:2048]]
        last_t = [0]

        def tern2048(buf, tau, m_tau_w, rpar, dsts):
            wait("vector", stt, reg_rel[rpar])
            wait("vector", sv, last_t[0])
            if m_tau_w:
                wait("vector", sv, m_tau_w)
            m_t1 = run("vector", lambda e, buf=buf, tau=tau: e.tensor_scalar(
                t1bf[:], buf[:], -1.0, tau[:], ALU.mult, ALU.is_gt), sv)
            wait("vector", sv, m_t1)
            m_t = run("vector", lambda e, buf=buf, tau=tau, rpar=rpar:
                      e.scalar_tensor_tensor(
                          wtbf[:, rpar * 2048:(rpar + 1) * 2048], buf[:],
                          tau[:], t1bf[:], ALU.is_gt, ALU.subtract), sv)
            m_trx = None
            for half in range(2):
                par = upar[0] % 4
                upar[0] += 1
                bank = banks[par]
                wait("tensor", sv, m_t)
                wait("tensor", sa, bank_rel[par])
                for k in range(8):
                    off = rpar * 2048 + half * 1024 + k * 128
                    m_trx = run("tensor", lambda e, bank=bank, k=k, off=off:
                                e.transpose(bank[:, k * 128:(k + 1) * 128],
                                            wtbf[:, off:off + 128],
                                            idb[:]), stt)
                wait("scalar", stt, m_trx)
                bank_rel[par] = dsts[half](bank)
            reg_rel[rpar] = m_trx
            last_t[0] = m_t
            return m_t

        def up_evac(fg):
            def f(bank, fg=fg):
                ss = fg % 4
                wait("scalar", swp[ss], stg_dma[ss])
                m_ev = run("scalar", lambda e, bank=bank, ss=ss: e.activation(
                    stg_slot[ss], bank[:], AF.Copy), sa)
                wait("scalar", sa, m_ev)
                wdst = bass.AP(wup_scr,
                               (fg // 2) * 128 * 2048 + (fg % 2) * 1024,
                               [[2048, 128], [128, NDT], [1, 128]])
                stg_dma[ss] = dma("scalar", swp[ss], wdst,
                                  stg_slot[ss].rearrange(
                                      "p (a b) -> p a b", a=NDT))
                return m_ev
            return f

        def dn_evac(j, ft0):
            def f(bank, j=j, ft0=ft0):
                return run("scalar", lambda e, bank=bank: e.activation(
                    WdnT[:, ft0:ft0 + 8, j * 128:(j + 1) * 128],
                    bank[:].rearrange("p (a b) -> p a b", a=8), AF.Copy), sa)
            return f

        m_abs = []
        m_sl = []
        m_gexp = []
        wait("scalar", sw, m_er)
        for i in range(16):
            srcd = bass.AP(wdn_d, (i // 2) * 128 * DFF + (i % 2) * 2048,
                           [[DFF, 128], [1, 2048]])
            bd = i % 2
            mldd = pool_load(bd, srcd, False)
            srcu = bass.AP(wup_d, i * 2 * 128 * D,
                           [[D, 128], [128 * D, 2], [1, D]])
            bu = 2 + i % 3
            mldu = pool_load(bu, srcu, True)
            wait("scalar", psem[bd], mldd)
            if m_abs:
                wait("scalar", sa, m_abs[-1])
            m_ab = run("scalar", lambda e, bufd=pool[bd], i=i: e.activation(
                hankf[:, 2048:4096], bufd[:], AF.Abs,
                accum_out=acw[:, 16 + i:17 + i]), sa)
            m_abs.append(m_ab)
            buf_rel[bd] = (sa, m_ab)
            wait("vector", psem[bu], mldu)
            m_t = tern2048(
                pool[bu], tau_u, m_tau_u if i == 0 else 0, i % 2,
                [up_evac(2 * i), up_evac(2 * i + 1)])
            buf_rel[bu] = (sv, m_t)
            if i >= 2:
                wait("scalar", sa, m_gexp[i - 2])
            esrc = bass.AP(erow_d, i * 1024, [[0, 128], [1, 1024]])
            m_sl.append(dma("scalar", sgE[i % 2], Ebc[:, i % 2, :], esrc))
            wait("scalar", sgE[i % 2], m_sl[i])
            if m_gexp:
                wait("scalar", sa, m_gexp[-1])
            m_ge = run("scalar", lambda e, na=i: e.activation(
                Msb[:], Ebc[:, na % 2, :], AF.Exp, scale=nkrev[:],
                accum_out=g_sb[:, na:na + 1]), sa)
            m_gexp.append(m_ge)
            flush_steps(1)

        wait("vector", sa, m_abs[15])
        m_rd = run("vector", lambda e: e.tensor_reduce(
            acw[:, 33:34], acw[:, 16:32], AX, ALU.add), sv)
        wait("tensor", sv, m_rd)
        m_ammd = run("tensor", lambda e: e.matmul(
            misc1[:, 1:2], onesf[:], acw[:, 33:34], start=True, stop=True), stt)
        wait("vector", stt, m_ammd)
        m_tau_d = run("vector", lambda e: e.tensor_scalar(
            tau_d[:], misc1[:, 1:2], 0.5 / (DFF * D), None, ALU.mult), sv)

        wait("vector", sa, m_gexp[-1])
        m_gbf = run("vector", lambda e: e.tensor_scalar(
            gbf[:], g_sb[:], 1.0 / D, None, ALU.mult), sv)
        m_z = run("vector", lambda e: e.memset(zrow[:], 0.0), sv)
        wait("tensor", sv, m_gbf)
        wait("tensor", sa, bank_rel[0])
        m_gtr = run("tensor", lambda e: e.transpose(
            trA[0:16, 0:128], gbf[:], idb[:]), stt)
        wait("vector", stt, m_gtr)
        m_gT = run("vector", lambda e: e.tensor_copy(gT[:], trA[0:16, 0:128]),
                   sv)
        wait("tensor", sv, m_gT)
        wait("scalar", sv, m_gT)
        dma("scalar", sw, bass.AP(gp_d, 128, [[384, DS], [1, 128]]), gT[:])
        dma("scalar", sw, bass.AP(gp_d, 0, [[384, DS], [1, 128]]), zrow[:])
        m_gw = dma("scalar", sw, bass.AP(gp_d, 256, [[384, DS], [1, 128]]),
                   zrow[:])
        wait("scalar", sw, m_gw)
        wait("scalar", swp[2], stg_dma[2])
        wait("scalar", swp[3], stg_dma[3])
        wait("scalar", sa, m_abs[15])
        for hq in range(4):
            hsrc = bass.AP(gp_d, hq * 4 * 384, [[1, 128], [384, 4], [1, 256]])
            m_hank = dma("scalar", sw, hank[:, hq * 4:(hq + 1) * 4, :], hsrc)
        wait("scalar", sw, m_hank)
        m_tall = run("scalar", lambda e: e.activation(
            T_all[:], hank[:, :, ::-1], AF.Copy), sa)

        for i in range(16):
            j, fh = i // 2, i % 2
            src = bass.AP(wdn_d, j * 128 * DFF + fh * 2048,
                          [[DFF, 128], [1, 2048]])
            b = i % 5
            mld = pool_load(b, src, False)
            wait("vector", psem[b], mld)
            m_t = tern2048(
                pool[b], tau_d, m_tau_d if i == 0 else 0, i % 2,
                [dn_evac(j, fh * 16), dn_evac(j, fh * 16 + 8)])
            buf_rel[b] = (sv, m_t)
            flush_steps(1)
        flush_all()

    PRO_V, PRO_A, PRO_T, PRO_W = sv.n, sa.n, stt.n, sw.n
    PRO_SWP = [swp[0].n, swp[1].n]
    PRO_G = sg.n
    for eng in ("sync", "scalar", "vector", "tensor"):
        wait(eng, sv, PRO_V)
        wait(eng, sa, PRO_A)
        wait(eng, stt, PRO_T)
        wait(eng, sw, PRO_W)
        wait(eng, swp[0], PRO_SWP[0])
        wait(eng, swp[1], PRO_SWP[1])
        wait(eng, swp[2], swp[2].n)
        wait(eng, swp[3], swp[3].n)
        wait(eng, sg, PRO_G)
        wait(eng, sgE[0], sgE[0].n)
        wait(eng, sgE[1], sgE[1].n)
        wait(eng, s_sm, SM)
        wait(eng, s_on, 16)

    xnT = sb("xnT", [128, NDT, CPS * 128], BF16)
    ln2_sb = sb("ln2", [128, CPS, D], BF16)
    h_sb = sb("h", [128, NFT, CPS * 128], BF16)
    out_sb = sb("out_sb", [128, 2, D], F32)
    tmpA = sb("tmpA", [128, D], F32)
    yf_sb = sb("yf", [128, CPS, D], F32)
    h2_sb = sb("h2", [128, 3, CPS * 128], BF16)
    hm_bf = sb("hm_bf", [128, CPS, DS], BF16)
    hmT = sb("hmT", [16, CPS * 128], BF16)
    qrow = sb("qrow", [1, 512], F32)
    mm1a = pa("mm1a", [128, 512], F32)
    mm1b = pa("mm1b", [128, 512], F32)
    o2set = [[pa("o2Alo", [128, 512], F32), pa("o2Ahi", [128, 512], F32)],
             [pa("o2Blo", [128, 512], F32), pa("o2Bhi", [128, 512], F32)]]
    trA_rel = sv.n

    for sc in range(NSC if STAGE >= 2 else 0):
        row = (sc * CPS) // 16
        j0 = (sc * CPS) % 16
        slot0 = row * 17 + 1 + j0

        m_xnT_ev = []
        m_bxe = []
        for c in range(CPS):
            cc = sc * CPS + c
            wait("tensor", sv, prev_xn[cc])
            wait("tensor", sv, trA_rel)
            m_tr = None
            for dt in range(NDT):
                m_tr = run("tensor", lambda e, dt=dt, c=c: e.transpose(
                    trA[:, dt * 128:(dt + 1) * 128],
                    xnbf[:, c, dt * 128:(dt + 1) * 128], idb[:]), stt)
            tr_mark[cc] = m_tr
            wait("vector", stt, m_tr)
            if c == 0:
                wait("vector", stt, prev_mm1_done)
            m_ev = run("vector", lambda e, c=c: e.tensor_copy(
                xnT[:, :, c * 128:(c + 1) * 128],
                trA[:].rearrange("p (a b) -> p a b", a=NDT)), sv)
            trA_rel = m_ev
            m_xnT_ev.append(m_ev)
            wait("tensor", sv, m_ev)
            if c == 0 and misc1_rel[0] is not None:
                wait("tensor", *misc1_rel)
            m_bx = None
            for dt in range(NDT):
                m_bx = run("tensor", lambda e, dt=dt, c=c: e.matmul(
                    misc1[:, c * 16:(c + 1) * 16],
                    xnT[:, dt, c * 128:(c + 1) * 128],
                    bT[:, dt, :], start=(dt == 0), stop=(dt == NDT - 1)), stt)
            wait("vector", stt, m_bx)
            slot = row * 17 + 1 + (cc % 16)
            m_bxev = run("vector", lambda e, c=c, cc=cc, slot=slot:
                         e.tensor_scalar(
                bxs[:, slot, :], misc1[:, c * 16:(c + 1) * 16],
                col(AC, cc), None, ALU.mult), sv)
            m_bxe.append(m_bxev)

        wait("tensor", sv, m_bxe[-1])
        m_cv = None
        for n in range(DS):
            run("tensor", lambda e, n=n: e.matmul(
                misc1[:, 256 + n * CPS:256 + (n + 1) * CPS],
                T_all[:, n, 0:128], bxs[:, slot0:slot0 + CPS, n],
                start=True, stop=False), None)
            m_cv = run("tensor", lambda e, n=n: e.matmul(
                misc1[:, 256 + n * CPS:256 + (n + 1) * CPS],
                T_all[:, n, 128:256], bxs[:, slot0 - 1:slot0 - 1 + CPS, n],
                start=False, stop=True), stt)
        wait("vector", stt, m_cv)
        m_hm = run("vector", lambda e: e.tensor_copy(
            hm_bf[:], misc1[:, 256:256 + DS * CPS].rearrange(
                "p (n c) -> p c n", n=DS)), sv)
        misc1_rel = (sv, m_hm)

        m_hte = 0
        ln2_args = []
        for c in range(CPS):
            cc = sc * CPS + c
            xslot = cc % XSLOTS
            oset = o2set[cc % 2]
            yfb = yf_sb[:, c, :]
            wait("tensor", sv, m_hm if c == 0 else m_hte)
            wait("tensor", sv, trA_rel)
            m_ht = run("tensor", lambda e, c=c: e.transpose(
                trA[0:16, c * 128:(c + 1) * 128], hm_bf[:, c, :], idb[:]), stt)
            wait("vector", stt, m_ht)
            m_hte = run("vector", lambda e, c=c: e.tensor_copy(
                hmT[:, c * 128:(c + 1) * 128],
                trA[0:16, c * 128:(c + 1) * 128]), sv)
            trA_rel = m_hte
            m_dxa = 0
            for hf in range(2):
                wait("tensor", sv, m_hte if hf == 0 else m_dxa)
                if hf == 0:
                    wait("tensor", sv, o2_rel[cc % 2])
                m_y = run("tensor", lambda e, c=c, hf=hf, oset=oset: e.matmul(
                    oset[hf][:], hmT[:, c * 128:(c + 1) * 128],
                    cT[:, hf * 512:(hf + 1) * 512], start=True, stop=True), stt)
                if c == 0 and hf == 0:
                    wait("vector", sv, last_asm_mark)
                dxm = run("vector", lambda e, xslot=xslot, hf=hf: e.tensor_mul(
                    tmpA[:, hf * 512:(hf + 1) * 512],
                    x_at[:, xslot, hf * 512:(hf + 1) * 512],
                    Dbc[:, hf * 512:(hf + 1) * 512]), sv)
                wait("vector", stt, m_y)
                wait("vector", sv, dxm)
                if hf == 0 and cc >= CPS:
                    wait("vector", sv, m_l2m.get(cc - CPS, 0))
                m_dxa = run("vector", lambda e, hf=hf, oset=oset, yfb=yfb:
                            e.tensor_add(
                    yfb[:, hf * 512:(hf + 1) * 512], oset[hf][:],
                    tmpA[:, hf * 512:(hf + 1) * 512]), sv)
            prev_dx[cc] = m_dxa
            ln2_args.append((m_dxa, yfb))

        stepq.append(ln2_gen(sc, ln2_args))
        if sc + 1 < NSC:
            emit_x_loads(sc + 1, range(2))
            stepq.append(attn_pre_gen(sc + 1, range(2)))

        if STAGE < 2:
            continue
        mm1banks = [mm1a, mm1b]
        m_h2 = [None] * NFT
        m_qmm = [0] * NFT
        wait("tensor", sv, m_xnT_ev[-1])
        if sc == 1:
            wait("tensor", *misc1_rel)

        def issue_wup(sup):
            slot = sup % 2
            wait("scalar", stt, wup_rel[slot])
            wsrc = bass.AP(wup_scr, sup * 128 * 2048,
                           [[2048, 128], [1, 2048]])
            wup_ld[sup] = dma("scalar", swp[slot],
                              wupb[:, slot, :, :, :].rearrange(
                                  "p a b c -> p (a b c)"), wsrc)

        def emit_qmm(fg):
            wait("tensor", sv, m_h2[fg])
            if fg == 0:
                wait("tensor", *misc1_rel)
            m_qmm[fg] = run("tensor", lambda e, fg=fg: e.matmul(
                misc1[0:1, 0:512], onesb[:], h2_sb[:, fg % 3, :],
                start=(fg == 0), stop=(fg == NFT - 1),
                skip_group_check=True), stt)

        issue_wup(0)
        issue_wup(1)
        for fg in range(NFT):
            sup, fi = fg // 2, fg % 2
            bank = mm1banks[fg % 2]
            wait("tensor", sa, mm1_rel[fg % 2])
            if fi == 0:
                wait("tensor", swp[sup % 2], wup_ld[sup])
            m_m1 = None
            for dt in range(NDT):
                m_m1 = run("tensor", lambda e, bank=bank, dt=dt, sup=sup,
                           fi=fi: e.matmul(
                    bank[:, 0:CPS * 128], wupb[:, sup % 2, fi, dt, :],
                    xnT[:, dt, :], start=(dt == 0), stop=(dt == NDT - 1)), stt)
            if fi == 1:
                wup_rel[sup % 2] = m_m1
            if fg == NFT - 1:
                prev_mm1_done = m_m1
            if fg >= 2:
                emit_qmm(fg - 2)
            wait("scalar", stt, m_m1)
            if fi == 1 and sup + 2 < NFT // 2:
                issue_wup(sup + 2)
            if sc > 0 and fg < 2:
                wait("scalar", stt, prev_mm2_done)
            m_g1 = run("scalar", lambda e, bank=bank, fg=fg: e.activation(
                h_sb[:, fg, :], bank[:, 0:CPS * 128], AF.Gelu), sa)
            mm1_rel[fg % 2] = m_g1
            wait("vector", sa, m_g1)
            if fg >= 3:
                wait("vector", stt, m_qmm[fg - 3])
            elif sc > 0:
                wait("vector", stt, prev_q_last)
            m_h2[fg] = run("vector", lambda e, fg=fg: e.tensor_mul(
                h2_sb[:, fg % 3, :], h_sb[:, fg, :], h_sb[:, fg, :]), sv)
            flush_steps(2)
        emit_qmm(NFT - 2)
        emit_qmm(NFT - 1)
        prev_q_last = m_qmm[NFT - 1]
        flush_all()

        QS, SQ = 18, 19
        m_sce = 0
        wait("vector", stt, m_qmm[NFT - 1])
        m_qrow = run("vector", lambda e: e.tensor_copy(
            qrow[:], misc1[0:1, 0:512]), sv)
        wait("tensor", sv, m_qrow)
        sqm = 0
        for c in range(CPS):
            m_qtr = run("tensor", lambda e, c=c: e.transpose(
                misc1[:, 320 + c:321 + c], qrow[0:1, c * 128:(c + 1) * 128],
                onef[:]), stt)
            wait("scalar", stt, m_qtr)
            sqm = run("scalar", lambda e, c=c: e.activation(
                stc[:, SQ + 24 * c:SQ + 24 * c + 1],
                misc1[:, 320 + c:321 + c], AF.Sqrt,
                scale=invDFF[:]), sa)
            wait("vector", sa, sqm)
            d1 = run("vector", lambda e, c=c: e.tensor_scalar(
                stc[:, QS + 24 * c:QS + 24 * c + 1],
                stc[:, SQ + 24 * c:SQ + 24 * c + 1], EPS2, None, ALU.add), sv)
            wait("vector", sv, d1)
            m_sce = run("vector", lambda e, c=c: e.reciprocal(
                scol[:, c:c + 1], stc[:, QS + 24 * c:QS + 24 * c + 1]), sv)
        misc1_rel = (sa, sqm)

        for c in range(CPS):
            cc = sc * CPS + c
            r, jj = cc // 16, cc % 16
            xslot = cc % XSLOTS
            ab = cc % 2
            oset = o2set[cc % 2]
            m_m2l = m_m2h = None
            wait("tensor", sv, o2_rel[cc % 2])
            wait("tensor", sv, prev_dx[cc])
            for fg in range(NFT):
                m_m2l = run("tensor", lambda e, c=c, fg=fg, oset=oset: e.matmul(
                    oset[0][:], h_sb[:, fg, c * 128:(c + 1) * 128],
                    WdnT[:, fg, 0:512],
                    start=(fg == 0), stop=(fg == NFT - 1),
                    skip_group_check=True), stt)
                m_m2h = run("tensor", lambda e, c=c, fg=fg, oset=oset: e.matmul(
                    oset[1][:], h_sb[:, fg, c * 128:(c + 1) * 128],
                    WdnT[:, fg, 512:1024],
                    start=(fg == 0), stop=(fg == NFT - 1),
                    skip_group_check=True), stt)
            if c == CPS - 1:
                prev_mm2_done = m_m2h
            if sc == 0 and c == 0:
                wait("vector", sg, m_be)
            wait("vector", sv, last_asm_mark)
            wait("vector", sv, m_l2m[cc])
            p1 = run("vector", lambda e, c=c: e.tensor_mul(
                tmpA[:], ln2_sb[:, c, :], gbc[:]), sv)
            wait("vector", sv, p1)
            p2 = run("vector", lambda e: e.tensor_add(tmpA[:], tmpA[:], bbc[:]),
                sv)
            wait("vector", sv, p2)
            p3 = run("vector", lambda e, xslot=xslot: e.tensor_add(
                tmpA[:], tmpA[:], x_at[:, xslot, :]), sv)
            wait("vector", sv, p3)
            if cc >= 2:
                wait("vector", soo[ab], m_out.get(cc - 2, 0))
            wait("vector", stt, m_m2h)
            run("vector", lambda e, c=c, ab=ab, oset=oset: e.scalar_tensor_tensor(
                out_sb[:, ab, 0:512], oset[0][:], scol[:, c:c + 1],
                tmpA[:, 0:512], ALU.mult, ALU.add), None)
            m_asm = run("vector", lambda e, c=c, ab=ab, oset=oset: e.scalar_tensor_tensor(
                out_sb[:, ab, 512:1024], oset[1][:], scol[:, c:c + 1],
                tmpA[:, 512:1024], ALU.mult, ALU.add), sv)
            o2_rel[cc % 2] = m_asm
            prev_asm[cc] = m_asm
            last_asm_mark = m_asm
            wait("sync", sv, m_asm)
            ydst = bass.AP(y_d, (r * SEQ + jj * 128) * D, [[D, 128], [1, D]])
            m_o = dma("sync", soo[ab], ydst, out_sb[:, ab, :])
            m_out[cc] = m_o

        if sc + 1 < NSC:
            emit_x_loads(sc + 1, [2, 3])
            stepq.append(attn_pre_gen(sc + 1, [2, 3]))
            flush_all()

    if STAGE >= 3:
        wait("sync", soo[0], soo[0].n)
        wait("sync", soo[1], soo[1].n)
    else:
        wait("sync", sv, sv.n)
        wait("sync", sa, sa.n)
        wait("sync", stt, stt.n)

    with nc.Block() as block:
        @block.sync
        def _(e):
            for f in ops["sync"]:
                f(e)

        @block.scalar
        def _(e):
            for f in ops["scalar"]:
                f(e)

        @block.vector
        def _(e):
            for f in ops["vector"]:
                f(e)

        @block.tensor
        def _(e):
            for f in ops["tensor"]:
                f(e)

        @block.gpsimd
        def _(e):
            for f in ops["gpsimd"]:
                f(e)

    return nc


_NC = None


def kernel(L0_attn_x, L0_attn_A_log, L0_attn_B_w, L0_attn_C_w, L0_attn_D,
           L0_attn_gamma, L0_attn_beta, L0_ffn_w_up, L0_ffn_w_down):
    global _NC
    f32 = np.float32
    x = np.ascontiguousarray(np.asarray(L0_attn_x, f32))
    wup = np.ascontiguousarray(np.asarray(L0_ffn_w_up, f32))
    wdn = np.ascontiguousarray(np.asarray(L0_ffn_w_down, f32))
    alog = np.ascontiguousarray(np.asarray(L0_attn_A_log, f32))
    bw = np.ascontiguousarray(np.asarray(L0_attn_B_w, f32))
    cw = np.ascontiguousarray(np.asarray(L0_attn_C_w, f32))
    dv = np.ascontiguousarray(np.asarray(L0_attn_D, f32)).reshape(1, D)
    ga = np.ascontiguousarray(np.asarray(L0_attn_gamma, f32)).reshape(1, D)
    be = np.ascontiguousarray(np.asarray(L0_attn_beta, f32)).reshape(1, D)

    idb = np.eye(128, dtype=ml_dtypes.bfloat16)
    idf = np.eye(128, dtype=f32)
    onesb = np.ones((128, 1), dtype=ml_dtypes.bfloat16)
    onesf = np.ones((128, 128), dtype=f32)
    nkrev = (np.arange(128, dtype=f32) - 127.0).reshape(128, 1)

    if _NC is None:
        _NC = build_nc()

    in_maps = [{
        "x": x[2 * core:2 * core + 2],
        "wup": wup, "wdn": wdn, "alog": alog, "bw": bw, "cw": cw,
        "dv": dv, "ga": ga, "be": be,
        "idb": idb, "idf": idf, "onesb": onesb, "onesf": onesf,
        "nkrev": nkrev,
    } for core in range(8)]
    res = run_bass_kernel_spmd(_NC, in_maps, core_ids=list(range(8)))
    return np.concatenate([r["y"] for r in res.results], axis=0)


# version 14
# speedup vs baseline: 1.0155x; 1.0155x over previous
import numpy as np
import ml_dtypes
from contextlib import ExitStack
import concourse.bass as bass
import concourse.mybir as mybir
from concourse.bass_utils import run_bass_kernel_spmd

F32 = mybir.dt.float32
BF16 = mybir.dt.bfloat16
AF = mybir.ActivationFunctionType
ALU = mybir.AluOpType
AX = mybir.AxisListType.X

D = 1024
DS = 16
SEQ = 2048
NCH = 32
CPS = 4
NSC = NCH // CPS
DFF = 4096
NFT = 32
NDT = 8
EPS1 = 1e-5
EPS2 = 1e-8


class Ctr:
    def __init__(self, sem):
        self.sem = sem
        self.n = 0


RACE_DEBUG = False
DETECT = False
STAGE = 3


def build_nc():
    nc = bass.Bass("TRN2", target_bir_lowering=False,
                   detect_race_conditions=RACE_DEBUG or DETECT)

    x_d = nc.dram_tensor("x", [2, SEQ, D], F32, kind="ExternalInput")
    wup_d = nc.dram_tensor("wup", [DFF, D], F32, kind="ExternalInput")
    wdn_d = nc.dram_tensor("wdn", [D, DFF], F32, kind="ExternalInput")
    alog_d = nc.dram_tensor("alog", [D, DS], F32, kind="ExternalInput")
    bw_d = nc.dram_tensor("bw", [DS, D], F32, kind="ExternalInput")
    cw_d = nc.dram_tensor("cw", [D, DS], F32, kind="ExternalInput")
    dv_d = nc.dram_tensor("dv", [1, D], F32, kind="ExternalInput")
    ga_d = nc.dram_tensor("ga", [1, D], F32, kind="ExternalInput")
    be_d = nc.dram_tensor("be", [1, D], F32, kind="ExternalInput")
    idb_d = nc.dram_tensor("idb", [128, 128], BF16, kind="ExternalInput")
    idf_d = nc.dram_tensor("idf", [128, 128], F32, kind="ExternalInput")
    onesb_d = nc.dram_tensor("onesb", [128, 1], BF16, kind="ExternalInput")
    onesf_d = nc.dram_tensor("onesf", [128, 128], F32, kind="ExternalInput")
    nkrev_d = nc.dram_tensor("nkrev", [128, 1], F32, kind="ExternalInput")
    y_d = nc.dram_tensor("y", [2, SEQ, D], F32, kind="ExternalOutput")
    erow_d = nc.dram_tensor("erow_scr", [1, DS * D], F32, kind="Internal")
    gp_d = nc.dram_tensor("gp_scr", [DS, 384], BF16, kind="Internal")
    wup_scr = nc.dram_tensor("wup_scr", [NFT // 2, 128, 2, NDT, 128], BF16,
                             kind="Internal")

    sb = nc.alloc_sbuf_tensor
    wupb = sb("wupb", [128, 2, 2, NDT, 128], BF16)
    WdnT = sb("WdnT", [128, NFT, D], BF16)
    T_all = sb("T_all", [128, DS, 256], BF16)
    bxs = sb("bxs", [128, 34, DS], BF16)
    bT = sb("bT", [128, NDT, DS], BF16)
    cT = sb("cT", [16, NDT * 128], BF16)
    Dbc = sb("Dbc", [128, D], F32)
    gbc = sb("gbc", [128, D], BF16)
    bbc = sb("bbc", [128, D], BF16)
    idb = sb("idb_s", [128, 128], BF16)
    onesb = sb("onesb_s", [128, 1], BF16)
    nkrev = sb("nkrev_s", [128, 1], F32)
    onef = sb("onef", [1, 1], F32)
    tau_u = sb("tau_u", [128, 1], F32)
    tau_d = sb("tau_d", [128, 1], F32)
    acw = sb("acw", [128, 34], F32)


    pa = nc.alloc_psum_tensor
    trA = pa("trA", [128, 1024], BF16)
    misc1 = pa("misc1", [128, 512], F32)

    s_sm = Ctr(nc.alloc_semaphore("s_sm"))
    s_on = Ctr(nc.alloc_semaphore("s_on"))
    sw = Ctr(nc.alloc_semaphore("sw"))
    sp1 = [Ctr(nc.alloc_semaphore("sp1a")), Ctr(nc.alloc_semaphore("sp1b"))]
    sp2 = [Ctr(nc.alloc_semaphore("sp2a")), Ctr(nc.alloc_semaphore("sp2b")),
           Ctr(nc.alloc_semaphore("sp2c"))]
    sxc = [Ctr(nc.alloc_semaphore(f"sxc{i}")) for i in range(7)]
    soo = [Ctr(nc.alloc_semaphore("soa")), Ctr(nc.alloc_semaphore("sob"))]
    swp = [Ctr(nc.alloc_semaphore("swpa")), Ctr(nc.alloc_semaphore("swpb")),
           Ctr(nc.alloc_semaphore("swpc")), Ctr(nc.alloc_semaphore("swpd"))]
    sgE = [Ctr(nc.alloc_semaphore("sgE0")), Ctr(nc.alloc_semaphore("sgE1"))]
    sv = Ctr(nc.alloc_semaphore("sv"))
    sa = Ctr(nc.alloc_semaphore("sa"))
    stt = Ctr(nc.alloc_semaphore("stt"))
    sg = Ctr(nc.alloc_semaphore("sg"))

    ops = {k: [] for k in ("sync", "scalar", "vector", "tensor", "gpsimd")}

    def em(eng, fn):
        ops[eng].append(fn)

    def dma(eng, ctr, out_ap, in_ap):
        def f(e, o=out_ap, i=in_ap, s=ctr.sem):
            e.dma_start(out=o, in_=i).then_inc(s, 16)
        em(eng, f)
        ctr.n += 16
        return ctr.n

    def wait(eng, ctr, v):
        if v <= 0:
            return
        s = ctr.sem if isinstance(ctr, Ctr) else ctr

        def f(e, s=s, v=v):
            e.wait_ge(s, v)
        em(eng, f)

    eng_ctr = {"vector": sv, "scalar": sa, "tensor": stt}

    def run(eng, fn, inc=None):
        if RACE_DEBUG and eng in eng_ctr:
            c = eng_ctr[eng]
            if c.n:
                wait(eng, c, c.n)
            inc = c

        if inc is None:
            em(eng, lambda e, fn=fn: fn(e))
            return None

        def f(e, fn=fn, s=inc.sem):
            fn(e).then_inc(s, 1)
        em(eng, f)
        inc.n += 1
        return inc.n

    x_at = sb("x_at", [128, 6, D], F32)
    xnbf = sb("xnbf", [128, CPS, D], BF16)
    stc = sb("stc", [128, 24 * CPS], F32)
    scol = sb("scol", [128, CPS], F32)
    invD = sb("invD", [128, 1], F32)
    invDFF = sb("invDFF", [128, 1], F32)
    (MUR, QQ, VARP, VEPS, T1, SVC, RR, AC, T0, C1, NB,
     M2R, Q2, V2P, V2E, T2, R2, N2) = range(18)

    o2_rel = [0, 0]
    misc1_rel = (None, 0)
    mm1_rel = [0, 0]
    prev_mm1_done = 0
    prev_mm2_done = 0
    prev_asm = {}
    prev_xn = {}
    prev_xq = {}
    prev_dx = {}
    m_l2m = {}
    tr_mark = {}
    junk_rel = [0]
    prev_q_last = 0
    m_out = {}
    m_xld = {}
    last_asm_mark = 0
    wup_ld = {}
    wup_rel = [0, 0]
    cst = [0]
    XSLOTS = 6

    def col(k, cc):
        c = cc % CPS
        return stc[:, k + 24 * c:k + 24 * c + 1]

    def emit_x_loads(sc, chunks=None):
        for c in (range(CPS) if chunks is None else chunks):
            cc = sc * CPS + c
            r, jj = cc // 16, cc % 16
            xslot = cc % XSLOTS
            if cc >= XSLOTS:
                wait("sync", sa, prev_xq.get(cc - XSLOTS, 0))
                wait("sync", sv, prev_asm.get(cc - XSLOTS, 0))
            xsrc = bass.AP(x_d, (r * SEQ + jj * 128) * D, [[D, 128], [1, D]])
            m_xld[cc] = dma("sync", sxc[xslot], x_at[:, xslot, :], xsrc)

    def attn_pre_gen(sc, chunks=None):
        chunks = list(range(CPS) if chunks is None else chunks)
        ccs = [sc * CPS + c for c in chunks]
        v1 = {}
        mq = {}
        vc = {}
        for c, cc in zip(chunks, ccs):
            xslot = cc % XSLOTS
            wait("vector", sxc[xslot], m_xld[cc])
            v1[cc] = run("vector", lambda e, xslot=xslot, cc=cc:
                         e.tensor_reduce(
                col(MUR, cc), x_at[:, xslot, :], AX, ALU.add), sv)
        yield
        for c, cc in zip(chunks, ccs):
            xslot = cc % XSLOTS
            wait("scalar", sxc[xslot], m_xld[cc])
            wait("scalar", sa, junk_rel[0])
            if cc >= CPS:
                wait("scalar", stt, tr_mark.get(cc - CPS, 0))
            m_q = run("scalar", lambda e, xslot=xslot, cc=cc, c=c:
                      e.activation(
                xnbf[:, c, :], x_at[:, xslot, :], AF.Square,
                accum_out=col(QQ, cc)), sa)
            junk_rel[0] = m_q
            prev_xq[cc] = m_q
            mq[cc] = m_q
        yield
        for c, cc in zip(chunks, ccs):
            wait("vector", sa, mq[cc])
            wait("vector", sv, v1[cc])
            v2a = run("vector", lambda e, cc=cc: e.scalar_tensor_tensor(
                col(T0, cc), col(MUR, cc), 1.0 / D, col(MUR, cc),
                ALU.mult, ALU.mult), sv)
            wait("vector", sv, v2a)
            v2b = run("vector", lambda e, cc=cc: e.tensor_sub(
                col(VARP, cc), col(QQ, cc), col(T0, cc)), sv)
            wait("vector", sv, v2b)
            vc[cc] = run("vector", lambda e, cc=cc: e.tensor_scalar(
                col(VEPS, cc), col(VARP, cc), 1.0 / D, EPS1,
                ALU.mult, ALU.add), sv)
        yield
        s2m = {}
        for c, cc in zip(chunks, ccs):
            wait("scalar", sv, vc[cc])
            if cc < CPS:
                wait("scalar", sv, cst[0])
            run("scalar", lambda e, cc=cc: e.activation(
                col(T1, cc), col(VEPS, cc), AF.Sqrt), sa)
            s2m[cc] = run("scalar", lambda e, cc=cc: e.activation(
                col(SVC, cc), col(VARP, cc), AF.Sqrt, scale=invD[:]), sa)
        yield
        v7 = {}
        for c, cc in zip(chunks, ccs):
            wait("vector", sa, s2m[cc])
            v3 = run("vector", lambda e, cc=cc: e.reciprocal(
                col(RR, cc), col(T1, cc)), sv)
            wait("vector", sv, v3)
            v4 = run("vector", lambda e, cc=cc: e.tensor_scalar(
                col(AC, cc), col(SVC, cc), col(RR, cc), EPS2,
                ALU.mult, ALU.add), sv)
            wait("vector", sv, v4)
            v5 = run("vector", lambda e, cc=cc: e.reciprocal(
                col(T0, cc), col(AC, cc)), sv)
            wait("vector", sv, v5)
            v6 = run("vector", lambda e, cc=cc: e.tensor_mul(
                col(C1, cc), col(RR, cc), col(T0, cc)), sv)
            wait("vector", sv, v6)
            v7[cc] = run("vector", lambda e, cc=cc: e.scalar_tensor_tensor(
                col(NB, cc), col(MUR, cc), -1.0 / D, col(C1, cc),
                ALU.mult, ALU.mult), sv)
        yield
        for c, cc in zip(chunks, ccs):
            xslot = cc % XSLOTS
            wait("vector", sv, v7[cc])
            wait("vector", sa, junk_rel[0])
            if cc >= CPS:
                wait("vector", stt, tr_mark.get(cc - CPS, 0))
            m_xn = run("vector", lambda e, xslot=xslot, cc=cc, c=c:
                       e.tensor_scalar(
                xnbf[:, c, :], x_at[:, xslot, :], col(C1, cc), col(NB, cc),
                ALU.mult, ALU.add), sv)
            prev_xn[cc] = m_xn
        yield

    def ln2_gen(sc, args):
        ccs = [sc * CPS + c for c in range(CPS)]
        w1 = {}
        q2 = {}
        tc = {}
        sq2 = {}
        for c, cc in zip(range(CPS), ccs):
            m_dxa, yfb = args[c]
            wait("vector", sv, m_dxa)
            w1[cc] = run("vector", lambda e, cc=cc, yfb=yfb: e.tensor_reduce(
                col(M2R, cc), yfb, AX, ALU.add), sv)
        yield
        for c, cc in zip(range(CPS), ccs):
            m_dxa, yfb = args[c]
            wait("scalar", sv, m_dxa)
            wait("scalar", sa, junk_rel[0])
            wait("scalar", stt, tr_mark[cc])
            q2[cc] = run("scalar", lambda e, cc=cc, c=c, yfb=yfb:
                         e.activation(
                xnbf[:, c, :], yfb, AF.Square, accum_out=col(Q2, cc)), sa)
            junk_rel[0] = q2[cc]
        yield
        for c, cc in zip(range(CPS), ccs):
            wait("vector", sa, q2[cc])
            wait("vector", sv, w1[cc])
            t2a = run("vector", lambda e, cc=cc: e.scalar_tensor_tensor(
                col(T2, cc), col(M2R, cc), 1.0 / D, col(M2R, cc),
                ALU.mult, ALU.mult), sv)
            wait("vector", sv, t2a)
            t2b = run("vector", lambda e, cc=cc: e.tensor_sub(
                col(V2P, cc), col(Q2, cc), col(T2, cc)), sv)
            wait("vector", sv, t2b)
            tc[cc] = run("vector", lambda e, cc=cc: e.tensor_scalar(
                col(V2E, cc), col(V2P, cc), 1.0 / D, EPS1,
                ALU.mult, ALU.add), sv)
        yield
        for c, cc in zip(range(CPS), ccs):
            wait("scalar", sv, tc[cc])
            sq2[cc] = run("scalar", lambda e, cc=cc: e.activation(
                col(T2, cc), col(V2E, cc), AF.Sqrt), sa)
        yield
        for c, cc in zip(range(CPS), ccs):
            m_dxa, yfb = args[c]
            wait("vector", sa, sq2[cc])
            r2m = run("vector", lambda e, cc=cc: e.reciprocal(
                col(R2, cc), col(T2, cc)), sv)
            wait("vector", sv, r2m)
            n2m = run("vector", lambda e, cc=cc: e.scalar_tensor_tensor(
                col(N2, cc), col(M2R, cc), -1.0 / D, col(R2, cc),
                ALU.mult, ALU.mult), sv)
            wait("vector", sv, n2m)
            m_l2 = run("vector", lambda e, c=c, cc=cc, yfb=yfb:
                       e.tensor_scalar(
                ln2_sb[:, c, :], yfb, col(R2, cc), col(N2, cc),
                ALU.mult, ALU.add), sv)
            m_l2m[cc] = m_l2
        yield

    stepq = []

    def flush_steps(n):
        done = 0
        while stepq and done < n:
            try:
                next(stepq[0])
                done += 1
            except StopIteration:
                stepq.pop(0)

    def flush_all():
        while stepq:
            flush_steps(1000)

    with ExitStack() as pro:
        psb = lambda name, shape, dt: pro.enter_context(
            nc.sbuf_tensor(name, shape, dt))
        onesf = psb("onesf_s", [128, 128], F32)
        idf = psb("idf_s", [128, 128], F32)
        Asb = psb("Asb", [128, 128], F32)
        Esb = psb("Esb", [128, 128], F32)
        EsbT = psb("EsbT", [128, 128], F32)
        Ebc = psb("Ebc", [128, 2, 1024], F32)
        Msb = psb("Msb", [128, 1024], F32)
        g_sb = psb("g_sb", [128, DS], F32)
        gbf = psb("gbf", [128, DS], BF16)
        gT = psb("gT", [16, 128], BF16)
        zrow = psb("zrow", [16, 128], BF16)
        hank = psb("hank", [128, DS, 256], BF16)
        Bbf = psb("Bbf", [16, D], BF16)
        Csb = psb("Csb", [128, NDT, DS], F32)
        Cbf = psb("Cbf", [128, NDT, DS], BF16)
        p1a = psb("p1a", [128, 2048], F32)
        p1b = psb("p1b", [128, 2048], F32)
        p2a = psb("p2a", [128, 2048], F32)
        p2b = psb("p2b", [128, 2048], F32)
        p2c = psb("p2c", [128, 2048], F32)
        t1bf = psb("t1bf", [128, 2048], BF16)
        wtbf = psb("wtbf", [128, 4096], BF16)
        wstg = psb("wstg", [128, 2, 1024], BF16)
        trB = pro.enter_context(nc.psum_tensor("trB", [128, 1024], BF16))
        trC = pro.enter_context(nc.psum_tensor("trC", [128, 1024], BF16))
        trD = pro.enter_context(nc.psum_tensor("trD", [128, 1024], BF16))

        Asb3 = Asb[:].rearrange("p (a b) -> p a b", a=NDT)
        for hq in range(2):
            asrc = bass.AP(alog_d, hq * 4 * 128 * DS,
                           [[DS, 128], [128 * DS, 4], [1, DS]])
            dma("sync", s_sm, Asb3[:, hq * 4:(hq + 1) * 4, :], asrc)
        for hq in range(2):
            csrc = bass.AP(cw_d, hq * 4 * 128 * DS,
                           [[DS, 128], [128 * DS, 4], [1, DS]])
            dma("sync", s_sm, Csb[:, hq * 4:(hq + 1) * 4, :], csrc)
        dma("sync", s_sm, Dbc[:], bass.AP(dv_d, 0, [[0, 128], [1, D]]))
        dma("sync", s_sm, idb[:], idb_d.ap())
        dma("sync", s_sm, idf[:], idf_d.ap())
        dma("sync", s_sm, onesb[:], onesb_d.ap())
        dma("sync", s_sm, nkrev[:], nkrev_d.ap())
        SM = s_sm.n
        dma("sync", s_on, onesf[:], onesf_d.ap())
        dma("gpsimd", sg, gbc[:], bass.AP(ga_d, 0, [[0, 128], [1, D]]))
        dma("gpsimd", sg, Bbf[:], bw_d.ap())
        m_be = dma("gpsimd", sg, bbc[:], bass.AP(be_d, 0, [[0, 128], [1, D]]))

        emit_x_loads(0)
        stepq.append(attn_pre_gen(0))

        run("vector", lambda e: e.memset(invD[:], 1.0 / D), sv)
        run("vector", lambda e: e.memset(onef[:], 1.0), sv)
        cst[0] = run("vector", lambda e: e.memset(invDFF[:], 1.0 / DFF), sv)
        wait("vector", s_sm, SM)
        m_cbf = run("vector", lambda e: e.tensor_copy(Cbf[:], Csb[:]), sv)
        run("vector", lambda e: e.memset(bxs[:], 0.0), sv)

        wait("scalar", s_sm, SM)
        m_exp = run("scalar", lambda e: e.activation(Esb[:], Asb[:], AF.Exp), sa)
        wait("tensor", s_sm, SM)
        wait("tensor", sa, m_exp)
        m_etr = run("tensor",
                    lambda e: e.transpose(misc1[:, 0:128], Esb[:], idf[:]), stt)
        wait("vector", stt, m_etr)
        m_etre = run("vector", lambda e: e.tensor_copy(EsbT[:], misc1[:, 0:128]),
                     sv)
        wait("scalar", sv, m_etre)
        for dt in range(NDT):
            edst = bass.AP(erow_d, dt * 128, [[D, DS], [1, 128]])
            m_er = dma("scalar", sw, edst, EsbT[dt * 16:(dt + 1) * 16, :])

        wait("tensor", sv, m_cbf)
        for dt in range(NDT):
            m_bctr = run("tensor", lambda e, dt=dt: e.transpose(
                trA[0:16, dt * 128:(dt + 1) * 128], Cbf[:, dt, :], idb[:]), stt)
        wait("vector", stt, m_bctr)
        m_cT = run("vector", lambda e: e.tensor_copy(cT[:], trA[0:16, :]), sv)
        wait("tensor", sv, m_cT)
        wait("tensor", sg, m_be)
        for dt in range(NDT):
            m_btr = run("tensor", lambda e, dt=dt: e.transpose(
                trA[0:128, dt * 16:(dt + 1) * 16],
                Bbf[:, dt * 128:(dt + 1) * 128], idb[0:16, 0:16]), stt)
        wait("vector", stt, m_btr)
        m_bT0 = run("vector", lambda e: e.tensor_copy(
            bT[:].rearrange("p a b -> p (a b)"), trA[0:128, 0:128]), sv)

        pool = [p1a, p1b, p2a, p2b, p2c]
        psem = [sp1[0], sp1[1], sp2[0], sp2[1], sp2[2]]
        buf_rel = {}

        def pool_load(b, src, rearr):
            if b in buf_rel:
                wait("sync", *buf_rel[b])
            dst = (pool[b][:].rearrange("p (a b) -> p a b", a=2) if rearr
                   else pool[b][:])
            return dma("sync", psem[b], dst, src)

        m_absA = []
        for i in range(16):
            src = bass.AP(wup_d, i * 2 * 128 * D,
                          [[D, 128], [128 * D, 2], [1, D]])
            b = i % 5
            mld = pool_load(b, src, True)
            wait("vector", psem[b], mld)
            m_ab = run("vector", lambda e, buf=pool[b], i=i: e.tensor_reduce(
                acw[:, i:i + 1], buf[:], AX, ALU.add,
                apply_absolute_value=True), sv)
            m_absA.append(m_ab)
            buf_rel[b] = (sv, m_ab)
            flush_steps(1)

        wait("vector", sv, m_absA[15])
        m_ru = run("vector", lambda e: e.tensor_reduce(
            acw[:, 32:33], acw[:, 0:16], AX, ALU.add), sv)
        wait("tensor", s_on, 16)
        wait("tensor", sv, m_ru)
        m_ammu = run("tensor", lambda e: e.matmul(
            misc1[:, 0:1], onesf[:], acw[:, 32:33], start=True, stop=True), stt)
        wait("vector", stt, m_ammu)
        m_tau_u = run("vector", lambda e: e.tensor_scalar(
            tau_u[:], misc1[:, 0:1], 0.5 / (DFF * D), None, ALU.mult), sv)

        banks = [trA, trB, trC, trD]
        bank_rel = [0, 0, 0, 0]
        reg_rel = [0, 0]
        upar = [0]
        stg_dma = [0, 0, 0, 0]
        hankf = hank[:].rearrange("p a b -> p (a b)")
        stg_slot = [wstg[:, 0, :], wstg[:, 1, :],
                    hankf[:, 0:1024], hankf[:, 1024:2048]]
        last_t = [0]

        def tern2048(buf, tau, m_tau_w, rpar, dsts):
            wait("vector", stt, reg_rel[rpar])
            wait("vector", sv, last_t[0])
            if m_tau_w:
                wait("vector", sv, m_tau_w)
            m_t1 = run("vector", lambda e, buf=buf, tau=tau: e.tensor_scalar(
                t1bf[:], buf[:], -1.0, tau[:], ALU.mult, ALU.is_gt), sv)
            wait("vector", sv, m_t1)
            m_t = run("vector", lambda e, buf=buf, tau=tau, rpar=rpar:
                      e.scalar_tensor_tensor(
                          wtbf[:, rpar * 2048:(rpar + 1) * 2048], buf[:],
                          tau[:], t1bf[:], ALU.is_gt, ALU.subtract), sv)
            m_trx = None
            for half in range(2):
                par = upar[0] % 4
                upar[0] += 1
                bank = banks[par]
                wait("tensor", sv, m_t)
                wait("tensor", sa, bank_rel[par])
                for k in range(8):
                    off = rpar * 2048 + half * 1024 + k * 128
                    m_trx = run("tensor", lambda e, bank=bank, k=k, off=off:
                                e.transpose(bank[:, k * 128:(k + 1) * 128],
                                            wtbf[:, off:off + 128],
                                            idb[:]), stt)
                wait("scalar", stt, m_trx)
                bank_rel[par] = dsts[half](bank)
            reg_rel[rpar] = m_trx
            last_t[0] = m_t
            return m_t

        def up_evac(fg):
            def f(bank, fg=fg):
                ss = fg % 4
                wait("scalar", swp[ss], stg_dma[ss])
                m_ev = run("scalar", lambda e, bank=bank, ss=ss: e.activation(
                    stg_slot[ss], bank[:], AF.Copy), sa)
                wait("scalar", sa, m_ev)
                wdst = bass.AP(wup_scr,
                               (fg // 2) * 128 * 2048 + (fg % 2) * 1024,
                               [[2048, 128], [128, NDT], [1, 128]])
                stg_dma[ss] = dma("scalar", swp[ss], wdst,
                                  stg_slot[ss].rearrange(
                                      "p (a b) -> p a b", a=NDT))
                return m_ev
            return f

        def dn_evac(j, ft0):
            def f(bank, j=j, ft0=ft0):
                return run("scalar", lambda e, bank=bank: e.activation(
                    WdnT[:, ft0:ft0 + 8, j * 128:(j + 1) * 128],
                    bank[:].rearrange("p (a b) -> p a b", a=8), AF.Copy), sa)
            return f

        m_abs = []
        m_sl = []
        m_gexp = []
        wait("scalar", sw, m_er)
        for i in range(16):
            srcd = bass.AP(wdn_d, (i // 2) * 128 * DFF + (i % 2) * 2048,
                           [[DFF, 128], [1, 2048]])
            bd = i % 2
            mldd = pool_load(bd, srcd, False)
            srcu = bass.AP(wup_d, i * 2 * 128 * D,
                           [[D, 128], [128 * D, 2], [1, D]])
            bu = 2 + i % 3
            mldu = pool_load(bu, srcu, True)
            wait("scalar", psem[bd], mldd)
            if m_abs:
                wait("scalar", sa, m_abs[-1])
            m_ab = run("scalar", lambda e, bufd=pool[bd], i=i: e.activation(
                hankf[:, 2048:4096], bufd[:], AF.Abs,
                accum_out=acw[:, 16 + i:17 + i]), sa)
            m_abs.append(m_ab)
            buf_rel[bd] = (sa, m_ab)
            wait("vector", psem[bu], mldu)
            m_t = tern2048(
                pool[bu], tau_u, m_tau_u if i == 0 else 0, i % 2,
                [up_evac(2 * i), up_evac(2 * i + 1)])
            buf_rel[bu] = (sv, m_t)
            if i >= 2:
                wait("scalar", sa, m_gexp[i - 2])
            esrc = bass.AP(erow_d, i * 1024, [[0, 128], [1, 1024]])
            m_sl.append(dma("scalar", sgE[i % 2], Ebc[:, i % 2, :], esrc))
            wait("scalar", sgE[i % 2], m_sl[i])
            if m_gexp:
                wait("scalar", sa, m_gexp[-1])
            m_ge = run("scalar", lambda e, na=i: e.activation(
                Msb[:], Ebc[:, na % 2, :], AF.Exp, scale=nkrev[:],
                accum_out=g_sb[:, na:na + 1]), sa)
            m_gexp.append(m_ge)
            flush_steps(1)

        wait("vector", sa, m_abs[15])
        m_rd = run("vector", lambda e: e.tensor_reduce(
            acw[:, 33:34], acw[:, 16:32], AX, ALU.add), sv)
        wait("tensor", sv, m_rd)
        m_ammd = run("tensor", lambda e: e.matmul(
            misc1[:, 1:2], onesf[:], acw[:, 33:34], start=True, stop=True), stt)
        wait("vector", stt, m_ammd)
        m_tau_d = run("vector", lambda e: e.tensor_scalar(
            tau_d[:], misc1[:, 1:2], 0.5 / (DFF * D), None, ALU.mult), sv)

        wait("vector", sa, m_gexp[-1])
        m_gbf = run("vector", lambda e: e.tensor_scalar(
            gbf[:], g_sb[:], 1.0 / D, None, ALU.mult), sv)
        m_z = run("vector", lambda e: e.memset(zrow[:], 0.0), sv)
        wait("tensor", sv, m_gbf)
        wait("tensor", sa, bank_rel[0])
        m_gtr = run("tensor", lambda e: e.transpose(
            trA[0:16, 0:128], gbf[:], idb[:]), stt)
        wait("vector", stt, m_gtr)
        m_gT = run("vector", lambda e: e.tensor_copy(gT[:], trA[0:16, 0:128]),
                   sv)
        wait("tensor", sv, m_gT)
        wait("scalar", sv, m_gT)
        dma("scalar", sw, bass.AP(gp_d, 128, [[384, DS], [1, 128]]), gT[:])
        dma("scalar", sw, bass.AP(gp_d, 0, [[384, DS], [1, 128]]), zrow[:])
        m_gw = dma("scalar", sw, bass.AP(gp_d, 256, [[384, DS], [1, 128]]),
                   zrow[:])
        wait("scalar", sw, m_gw)
        wait("scalar", swp[2], stg_dma[2])
        wait("scalar", swp[3], stg_dma[3])
        wait("scalar", sa, m_abs[15])
        for hq in range(4):
            hsrc = bass.AP(gp_d, hq * 4 * 384, [[1, 128], [384, 4], [1, 256]])
            m_hank = dma("scalar", sw, hank[:, hq * 4:(hq + 1) * 4, :], hsrc)
        wait("scalar", sw, m_hank)
        m_tall = run("scalar", lambda e: e.activation(
            T_all[:], hank[:, :, ::-1], AF.Copy), sa)

        for i in range(16):
            j, fh = i // 2, i % 2
            src = bass.AP(wdn_d, j * 128 * DFF + fh * 2048,
                          [[DFF, 128], [1, 2048]])
            b = i % 5
            mld = pool_load(b, src, False)
            wait("vector", psem[b], mld)
            m_t = tern2048(
                pool[b], tau_d, m_tau_d if i == 0 else 0, i % 2,
                [dn_evac(j, fh * 16), dn_evac(j, fh * 16 + 8)])
            buf_rel[b] = (sv, m_t)
            flush_steps(1)
        flush_all()

    PRO_V, PRO_A, PRO_T, PRO_W = sv.n, sa.n, stt.n, sw.n
    PRO_SWP = [swp[0].n, swp[1].n]
    PRO_G = sg.n
    for eng in ("sync", "scalar", "vector", "tensor"):
        wait(eng, sv, PRO_V)
        wait(eng, sa, PRO_A)
        wait(eng, stt, PRO_T)
        wait(eng, sw, PRO_W)
        wait(eng, swp[0], PRO_SWP[0])
        wait(eng, swp[1], PRO_SWP[1])
        wait(eng, swp[2], swp[2].n)
        wait(eng, swp[3], swp[3].n)
        wait(eng, sg, PRO_G)
        wait(eng, sgE[0], sgE[0].n)
        wait(eng, sgE[1], sgE[1].n)
        wait(eng, s_sm, SM)
        wait(eng, s_on, 16)

    xnT = sb("xnT", [128, NDT, CPS * 128], BF16)
    ln2_sb = sb("ln2", [128, CPS, D], BF16)
    h_sb = sb("h", [128, NFT, CPS * 128], BF16)
    out_sb = sb("out_sb", [128, 2, D], F32)
    tmpA = sb("tmpA", [128, D], F32)
    yf_sb = sb("yf", [128, CPS, D], F32)
    h2_sb = sb("h2", [128, 3, CPS * 128], BF16)
    hm_bf = sb("hm_bf", [128, CPS, DS], BF16)
    hmT = sb("hmT", [16, CPS * 128], BF16)
    qrow = sb("qrow", [1, 512], F32)
    mm1a = pa("mm1a", [128, 512], F32)
    mm1b = pa("mm1b", [128, 512], F32)
    o2set = [[pa("o2Alo", [128, 512], F32), pa("o2Ahi", [128, 512], F32)],
             [pa("o2Blo", [128, 512], F32), pa("o2Bhi", [128, 512], F32)]]
    trA_rel = sv.n

    for sc in range(NSC if STAGE >= 2 else 0):
        row = (sc * CPS) // 16
        j0 = (sc * CPS) % 16
        slot0 = row * 17 + 1 + j0

        m_xnT_ev = []
        m_bxe = []
        for c in range(CPS):
            cc = sc * CPS + c
            wait("tensor", sv, prev_xn[cc])
            wait("tensor", sv, trA_rel)
            m_tr = None
            for dt in range(NDT):
                m_tr = run("tensor", lambda e, dt=dt, c=c: e.transpose(
                    trA[:, dt * 128:(dt + 1) * 128],
                    xnbf[:, c, dt * 128:(dt + 1) * 128], idb[:]), stt)
            tr_mark[cc] = m_tr
            wait("vector", stt, m_tr)
            if c == 0:
                wait("vector", stt, prev_mm1_done)
            m_ev = run("vector", lambda e, c=c: e.tensor_copy(
                xnT[:, :, c * 128:(c + 1) * 128],
                trA[:].rearrange("p (a b) -> p a b", a=NDT)), sv)
            trA_rel = m_ev
            m_xnT_ev.append(m_ev)
            wait("tensor", sv, m_ev)
            if c == 0 and misc1_rel[0] is not None:
                wait("tensor", *misc1_rel)
            m_bx = None
            for dt in range(NDT):
                m_bx = run("tensor", lambda e, dt=dt, c=c: e.matmul(
                    misc1[:, c * 16:(c + 1) * 16],
                    xnT[:, dt, c * 128:(c + 1) * 128],
                    bT[:, dt, :], start=(dt == 0), stop=(dt == NDT - 1)), stt)
            wait("vector", stt, m_bx)
            slot = row * 17 + 1 + (cc % 16)
            m_bxev = run("vector", lambda e, c=c, cc=cc, slot=slot:
                         e.tensor_scalar(
                bxs[:, slot, :], misc1[:, c * 16:(c + 1) * 16],
                col(AC, cc), None, ALU.mult), sv)
            m_bxe.append(m_bxev)

        wait("tensor", sv, m_bxe[-1])
        m_cv = None
        for n in range(DS):
            run("tensor", lambda e, n=n: e.matmul(
                misc1[:, 256 + n * CPS:256 + (n + 1) * CPS],
                T_all[:, n, 0:128], bxs[:, slot0:slot0 + CPS, n],
                start=True, stop=False), None)
            m_cv = run("tensor", lambda e, n=n: e.matmul(
                misc1[:, 256 + n * CPS:256 + (n + 1) * CPS],
                T_all[:, n, 128:256], bxs[:, slot0 - 1:slot0 - 1 + CPS, n],
                start=False, stop=True), stt)
        wait("vector", stt, m_cv)
        m_hm = run("vector", lambda e: e.tensor_copy(
            hm_bf[:], misc1[:, 256:256 + DS * CPS].rearrange(
                "p (n c) -> p c n", n=DS)), sv)
        misc1_rel = (sv, m_hm)

        m_hte = 0
        ln2_args = []
        for c in range(CPS):
            cc = sc * CPS + c
            xslot = cc % XSLOTS
            oset = o2set[cc % 2]
            yfb = yf_sb[:, c, :]
            wait("tensor", sv, m_hm if c == 0 else m_hte)
            wait("tensor", sv, trA_rel)
            m_ht = run("tensor", lambda e, c=c: e.transpose(
                trA[0:16, c * 128:(c + 1) * 128], hm_bf[:, c, :], idb[:]), stt)
            wait("vector", stt, m_ht)
            m_hte = run("vector", lambda e, c=c: e.tensor_copy(
                hmT[:, c * 128:(c + 1) * 128],
                trA[0:16, c * 128:(c + 1) * 128]), sv)
            trA_rel = m_hte
            m_dxa = 0
            for hf in range(2):
                wait("tensor", sv, m_hte if hf == 0 else m_dxa)
                if hf == 0:
                    wait("tensor", sv, o2_rel[cc % 2])
                m_y = run("tensor", lambda e, c=c, hf=hf, oset=oset: e.matmul(
                    oset[hf][:], hmT[:, c * 128:(c + 1) * 128],
                    cT[:, hf * 512:(hf + 1) * 512], start=True, stop=True), stt)
                if c == 0 and hf == 0:
                    wait("vector", sv, last_asm_mark)
                dxm = run("vector", lambda e, xslot=xslot, hf=hf: e.tensor_mul(
                    tmpA[:, hf * 512:(hf + 1) * 512],
                    x_at[:, xslot, hf * 512:(hf + 1) * 512],
                    Dbc[:, hf * 512:(hf + 1) * 512]), sv)
                wait("vector", stt, m_y)
                wait("vector", sv, dxm)
                if hf == 0 and cc >= CPS:
                    wait("vector", sv, m_l2m.get(cc - CPS, 0))
                m_dxa = run("vector", lambda e, hf=hf, oset=oset, yfb=yfb:
                            e.tensor_add(
                    yfb[:, hf * 512:(hf + 1) * 512], oset[hf][:],
                    tmpA[:, hf * 512:(hf + 1) * 512]), sv)
            prev_dx[cc] = m_dxa
            ln2_args.append((m_dxa, yfb))

        stepq.append(ln2_gen(sc, ln2_args))
        if sc + 1 < NSC:
            emit_x_loads(sc + 1, range(2))
            stepq.append(attn_pre_gen(sc + 1, range(2)))

        if STAGE < 2:
            continue
        mm1banks = [mm1a, mm1b]
        m_h2 = [None] * NFT
        m_qmm = [0] * NFT
        wait("tensor", sv, m_xnT_ev[-1])
        if sc == 1:
            wait("tensor", *misc1_rel)

        def issue_wup(sup):
            slot = sup % 2
            wait("scalar", stt, wup_rel[slot])
            wsrc = bass.AP(wup_scr, sup * 128 * 2048,
                           [[2048, 128], [1, 2048]])
            wup_ld[sup] = dma("scalar", swp[slot],
                              wupb[:, slot, :, :, :].rearrange(
                                  "p a b c -> p (a b c)"), wsrc)

        def emit_qmm(fg):
            wait("tensor", sv, m_h2[fg])
            if fg == 0:
                wait("tensor", *misc1_rel)
            m_qmm[fg] = run("tensor", lambda e, fg=fg: e.matmul(
                misc1[0:1, 0:512], onesb[:], h2_sb[:, fg % 3, :],
                start=(fg == 0), stop=(fg == NFT - 1),
                skip_group_check=True), stt)

        issue_wup(0)
        issue_wup(1)
        for fg in range(NFT):
            sup, fi = fg // 2, fg % 2
            bank = mm1banks[fg % 2]
            wait("tensor", sa, mm1_rel[fg % 2])
            if fi == 0:
                wait("tensor", swp[sup % 2], wup_ld[sup])
            m_m1 = None
            for dt in range(NDT):
                m_m1 = run("tensor", lambda e, bank=bank, dt=dt, sup=sup,
                           fi=fi: e.matmul(
                    bank[:, 0:CPS * 128], wupb[:, sup % 2, fi, dt, :],
                    xnT[:, dt, :], start=(dt == 0), stop=(dt == NDT - 1)), stt)
            if fi == 1:
                wup_rel[sup % 2] = m_m1
            if fg == NFT - 1:
                prev_mm1_done = m_m1
            if fg >= 2:
                emit_qmm(fg - 2)
            wait("scalar", stt, m_m1)
            if fi == 1 and sup + 2 < NFT // 2:
                issue_wup(sup + 2)
            if sc > 0 and fg < 2:
                wait("scalar", stt, prev_mm2_done)
            m_g1 = run("scalar", lambda e, bank=bank, fg=fg: e.activation(
                h_sb[:, fg, :], bank[:, 0:CPS * 128], AF.Gelu), sa)
            mm1_rel[fg % 2] = m_g1
            wait("vector", sa, m_g1)
            if fg >= 3:
                wait("vector", stt, m_qmm[fg - 3])
            elif sc > 0:
                wait("vector", stt, prev_q_last)
            m_h2[fg] = run("vector", lambda e, fg=fg: e.tensor_mul(
                h2_sb[:, fg % 3, :], h_sb[:, fg, :], h_sb[:, fg, :]), sv)
            flush_steps(2)
        emit_qmm(NFT - 2)
        emit_qmm(NFT - 1)
        prev_q_last = m_qmm[NFT - 1]
        flush_all()

        QS, SQ = 18, 19
        m_sce = 0
        wait("vector", stt, m_qmm[NFT - 1])
        m_qrow = run("vector", lambda e: e.tensor_copy(
            qrow[:], misc1[0:1, 0:512]), sv)
        wait("tensor", sv, m_qrow)
        sqm = 0
        for c in range(CPS):
            m_qtr = run("tensor", lambda e, c=c: e.transpose(
                misc1[:, 320 + c:321 + c], qrow[0:1, c * 128:(c + 1) * 128],
                onef[:]), stt)
            wait("scalar", stt, m_qtr)
            sqm = run("scalar", lambda e, c=c: e.activation(
                stc[:, SQ + 24 * c:SQ + 24 * c + 1],
                misc1[:, 320 + c:321 + c], AF.Sqrt,
                scale=invDFF[:]), sa)
            wait("vector", sa, sqm)
            d1 = run("vector", lambda e, c=c: e.tensor_scalar(
                stc[:, QS + 24 * c:QS + 24 * c + 1],
                stc[:, SQ + 24 * c:SQ + 24 * c + 1], EPS2, None, ALU.add), sv)
            wait("vector", sv, d1)
            m_sce = run("vector", lambda e, c=c: e.reciprocal(
                scol[:, c:c + 1], stc[:, QS + 24 * c:QS + 24 * c + 1]), sv)
        misc1_rel = (sa, sqm)

        for c in range(CPS):
            cc = sc * CPS + c
            r, jj = cc // 16, cc % 16
            xslot = cc % XSLOTS
            ab = cc % 2
            oset = o2set[cc % 2]
            m_m2l = m_m2h = None
            wait("tensor", sv, o2_rel[cc % 2])
            wait("tensor", sv, prev_dx[cc])
            for fg in range(NFT):
                m_m2l = run("tensor", lambda e, c=c, fg=fg, oset=oset: e.matmul(
                    oset[0][:], h_sb[:, fg, c * 128:(c + 1) * 128],
                    WdnT[:, fg, 0:512],
                    start=(fg == 0), stop=(fg == NFT - 1),
                    skip_group_check=True), stt)
                m_m2h = run("tensor", lambda e, c=c, fg=fg, oset=oset: e.matmul(
                    oset[1][:], h_sb[:, fg, c * 128:(c + 1) * 128],
                    WdnT[:, fg, 512:1024],
                    start=(fg == 0), stop=(fg == NFT - 1),
                    skip_group_check=True), stt)
            if c == CPS - 1:
                prev_mm2_done = m_m2h
            if sc == 0 and c == 0:
                wait("vector", sg, m_be)
            wait("vector", sv, last_asm_mark)
            wait("vector", sv, m_l2m[cc])
            p1 = run("vector", lambda e, c=c: e.tensor_mul(
                tmpA[:], ln2_sb[:, c, :], gbc[:]), sv)
            wait("vector", sv, p1)
            p2 = run("vector", lambda e: e.tensor_add(tmpA[:], tmpA[:], bbc[:]),
                sv)
            wait("vector", sv, p2)
            p3 = run("vector", lambda e, xslot=xslot: e.tensor_add(
                tmpA[:], tmpA[:], x_at[:, xslot, :]), sv)
            wait("vector", sv, p3)
            if cc >= 2:
                wait("vector", soo[ab], m_out.get(cc - 2, 0))
            wait("vector", stt, m_m2h)
            run("vector", lambda e, c=c, ab=ab, oset=oset: e.scalar_tensor_tensor(
                out_sb[:, ab, 0:512], oset[0][:], scol[:, c:c + 1],
                tmpA[:, 0:512], ALU.mult, ALU.add), None)
            m_asm = run("vector", lambda e, c=c, ab=ab, oset=oset: e.scalar_tensor_tensor(
                out_sb[:, ab, 512:1024], oset[1][:], scol[:, c:c + 1],
                tmpA[:, 512:1024], ALU.mult, ALU.add), sv)
            o2_rel[cc % 2] = m_asm
            prev_asm[cc] = m_asm
            last_asm_mark = m_asm
            wait("sync", sv, m_asm)
            ydst = bass.AP(y_d, (r * SEQ + jj * 128) * D, [[D, 128], [1, D]])
            m_o = dma("sync", soo[ab], ydst, out_sb[:, ab, :])
            m_out[cc] = m_o

        if sc + 1 < NSC:
            emit_x_loads(sc + 1, [2, 3])
            stepq.append(attn_pre_gen(sc + 1, [2, 3]))
            flush_all()

    if STAGE >= 3:
        wait("sync", soo[0], soo[0].n)
        wait("sync", soo[1], soo[1].n)
    else:
        wait("sync", sv, sv.n)
        wait("sync", sa, sa.n)
        wait("sync", stt, stt.n)

    with nc.Block() as block:
        @block.sync
        def _(e):
            for f in ops["sync"]:
                f(e)

        @block.scalar
        def _(e):
            for f in ops["scalar"]:
                f(e)

        @block.vector
        def _(e):
            for f in ops["vector"]:
                f(e)

        @block.tensor
        def _(e):
            for f in ops["tensor"]:
                f(e)

        @block.gpsimd
        def _(e):
            for f in ops["gpsimd"]:
                f(e)

    return nc


_NC = None


def kernel(L0_attn_x, L0_attn_A_log, L0_attn_B_w, L0_attn_C_w, L0_attn_D,
           L0_attn_gamma, L0_attn_beta, L0_ffn_w_up, L0_ffn_w_down):
    global _NC
    f32 = np.float32
    x = np.ascontiguousarray(np.asarray(L0_attn_x, f32))
    wup = np.ascontiguousarray(np.asarray(L0_ffn_w_up, f32))
    wdn = np.ascontiguousarray(np.asarray(L0_ffn_w_down, f32))
    alog = np.ascontiguousarray(np.asarray(L0_attn_A_log, f32))
    bw = np.ascontiguousarray(np.asarray(L0_attn_B_w, f32))
    cw = np.ascontiguousarray(np.asarray(L0_attn_C_w, f32))
    dv = np.ascontiguousarray(np.asarray(L0_attn_D, f32)).reshape(1, D)
    ga = np.ascontiguousarray(np.asarray(L0_attn_gamma, f32)).reshape(1, D)
    be = np.ascontiguousarray(np.asarray(L0_attn_beta, f32)).reshape(1, D)

    idb = np.eye(128, dtype=ml_dtypes.bfloat16)
    idf = np.eye(128, dtype=f32)
    onesb = np.ones((128, 1), dtype=ml_dtypes.bfloat16)
    onesf = np.ones((128, 128), dtype=f32)
    nkrev = (np.arange(128, dtype=f32) - 127.0).reshape(128, 1)

    if _NC is None:
        _NC = build_nc()

    in_maps = [{
        "x": x[2 * core:2 * core + 2],
        "wup": wup, "wdn": wdn, "alog": alog, "bw": bw, "cw": cw,
        "dv": dv, "ga": ga, "be": be,
        "idb": idb, "idf": idf, "onesb": onesb, "onesf": onesf,
        "nkrev": nkrev,
    } for core in range(8)]
    res = run_bass_kernel_spmd(_NC, in_maps, core_ids=list(range(8)))
    return np.concatenate([r["y"] for r in res.results], axis=0)


# version 15
# speedup vs baseline: 1.0540x; 1.0379x over previous
import numpy as np
import ml_dtypes
from contextlib import ExitStack
import concourse.bass as bass
import concourse.mybir as mybir
from concourse.bass_utils import run_bass_kernel_spmd

F32 = mybir.dt.float32
BF16 = mybir.dt.bfloat16
AF = mybir.ActivationFunctionType
ALU = mybir.AluOpType
AX = mybir.AxisListType.X

D = 1024
DS = 16
SEQ = 2048
NCH = 32
CPS = 4
NSC = NCH // CPS
DFF = 4096
NFT = 32
NDT = 8
EPS1 = 1e-5
EPS2 = 1e-8


class Ctr:
    def __init__(self, sem):
        self.sem = sem
        self.n = 0


RACE_DEBUG = False
DETECT = False
STAGE = 3


def build_nc():
    nc = bass.Bass("TRN2", target_bir_lowering=False,
                   detect_race_conditions=RACE_DEBUG or DETECT)

    x_d = nc.dram_tensor("x", [2, SEQ, D], F32, kind="ExternalInput")
    wup_d = nc.dram_tensor("wup", [DFF, D], F32, kind="ExternalInput")
    wdn_d = nc.dram_tensor("wdn", [D, DFF], F32, kind="ExternalInput")
    alog_d = nc.dram_tensor("alog", [D, DS], F32, kind="ExternalInput")
    bw_d = nc.dram_tensor("bw", [DS, D], F32, kind="ExternalInput")
    cw_d = nc.dram_tensor("cw", [D, DS], F32, kind="ExternalInput")
    dv_d = nc.dram_tensor("dv", [1, D], F32, kind="ExternalInput")
    ga_d = nc.dram_tensor("ga", [1, D], F32, kind="ExternalInput")
    be_d = nc.dram_tensor("be", [1, D], F32, kind="ExternalInput")
    idb_d = nc.dram_tensor("idb", [128, 128], BF16, kind="ExternalInput")
    idf_d = nc.dram_tensor("idf", [128, 128], F32, kind="ExternalInput")
    onesb_d = nc.dram_tensor("onesb", [128, 1], BF16, kind="ExternalInput")
    onesf_d = nc.dram_tensor("onesf", [128, 128], F32, kind="ExternalInput")
    nkrev_d = nc.dram_tensor("nkrev", [128, 1], F32, kind="ExternalInput")
    y_d = nc.dram_tensor("y", [2, SEQ, D], F32, kind="ExternalOutput")
    erow_d = nc.dram_tensor("erow_scr", [1, DS * D], F32, kind="Internal")
    gp_d = nc.dram_tensor("gp_scr", [DS, 384], BF16, kind="Internal")
    wup_scr = nc.dram_tensor("wup_scr", [NFT // 2, 128, 2, NDT, 128], BF16,
                             kind="Internal")

    sb = nc.alloc_sbuf_tensor
    wupb = sb("wupb", [128, 2, 2, NDT, 128], BF16)
    WdnT = sb("WdnT", [128, NFT, D], BF16)
    T_all = sb("T_all", [128, DS, 256], BF16)
    bxs = sb("bxs", [128, 34, DS], BF16)
    bT = sb("bT", [128, NDT, DS], BF16)
    cT = sb("cT", [16, NDT * 128], BF16)
    Dbc = sb("Dbc", [128, D], F32)
    gbc = sb("gbc", [128, D], BF16)
    bbc = sb("bbc", [128, D], BF16)
    idb = sb("idb_s", [128, 128], BF16)
    onesb = sb("onesb_s", [128, 1], BF16)
    nkrev = sb("nkrev_s", [128, 1], F32)
    onef = sb("onef", [1, 1], F32)
    tau_u = sb("tau_u", [128, 1], F32)
    tau_d = sb("tau_d", [128, 1], F32)
    acw = sb("acw", [128, 34], F32)


    pa = nc.alloc_psum_tensor
    trA = pa("trA", [128, 1024], BF16)
    misc1 = pa("misc1", [128, 512], F32)

    s_sm = Ctr(nc.alloc_semaphore("s_sm"))
    s_on = Ctr(nc.alloc_semaphore("s_on"))
    sw = Ctr(nc.alloc_semaphore("sw"))
    sp1 = [Ctr(nc.alloc_semaphore("sp1a")), Ctr(nc.alloc_semaphore("sp1b"))]
    sp2 = [Ctr(nc.alloc_semaphore("sp2a")), Ctr(nc.alloc_semaphore("sp2b")),
           Ctr(nc.alloc_semaphore("sp2c"))]
    sxc = [Ctr(nc.alloc_semaphore(f"sxc{i}")) for i in range(7)]
    soo = [Ctr(nc.alloc_semaphore("soa")), Ctr(nc.alloc_semaphore("sob"))]
    swp = [Ctr(nc.alloc_semaphore("swpa")), Ctr(nc.alloc_semaphore("swpb")),
           Ctr(nc.alloc_semaphore("swpc")), Ctr(nc.alloc_semaphore("swpd"))]
    sgE = [Ctr(nc.alloc_semaphore("sgE0")), Ctr(nc.alloc_semaphore("sgE1"))]
    sv = Ctr(nc.alloc_semaphore("sv"))
    sa = Ctr(nc.alloc_semaphore("sa"))
    stt = Ctr(nc.alloc_semaphore("stt"))
    sg = Ctr(nc.alloc_semaphore("sg"))

    ops = {k: [] for k in ("sync", "scalar", "vector", "tensor", "gpsimd")}

    def em(eng, fn):
        ops[eng].append(fn)

    def dma(eng, ctr, out_ap, in_ap):
        def f(e, o=out_ap, i=in_ap, s=ctr.sem):
            e.dma_start(out=o, in_=i).then_inc(s, 16)
        em(eng, f)
        ctr.n += 16
        return ctr.n

    def wait(eng, ctr, v):
        if v <= 0:
            return
        s = ctr.sem if isinstance(ctr, Ctr) else ctr

        def f(e, s=s, v=v):
            e.wait_ge(s, v)
        em(eng, f)

    eng_ctr = {"vector": sv, "scalar": sa, "tensor": stt}

    def run(eng, fn, inc=None):
        if RACE_DEBUG and eng in eng_ctr:
            c = eng_ctr[eng]
            if c.n:
                wait(eng, c, c.n)
            inc = c

        if inc is None:
            em(eng, lambda e, fn=fn: fn(e))
            return None

        def f(e, fn=fn, s=inc.sem):
            fn(e).then_inc(s, 1)
        em(eng, f)
        inc.n += 1
        return inc.n

    x_at = sb("x_at", [128, 6, D], F32)
    xnbf = sb("xnbf", [128, CPS, D], BF16)
    stc = sb("stc", [128, 24 * CPS], F32)
    scol = sb("scol", [128, CPS], F32)
    invD = sb("invD", [128, 1], F32)
    invDFF = sb("invDFF", [128, 1], F32)
    (MUR, QQ, VARP, VEPS, T1, SVC, RR, AC, T0, C1, NB,
     M2R, Q2, V2P, V2E, T2, R2, N2) = range(18)

    o2_rel = [0, 0]
    misc1_rel = (None, 0)
    mm1_rel = [0, 0]
    prev_mm1_done = 0
    prev_mm2_done = 0
    prev_asm = {}
    prev_xn = {}
    prev_xq = {}
    prev_dx = {}
    m_l2m = {}
    tr_mark = {}
    junk_rel = [0]
    prev_q_last = 0
    m_out = {}
    m_xld = {}
    last_asm_mark = 0
    wup_ld = {}
    wup_rel = [0, 0]
    cst = [0]
    XSLOTS = 6

    def col(k, cc):
        c = cc % CPS
        return stc[:, k + 24 * c:k + 24 * c + 1]

    def emit_x_loads(sc, chunks=None):
        for c in (range(CPS) if chunks is None else chunks):
            cc = sc * CPS + c
            r, jj = cc // 16, cc % 16
            xslot = cc % XSLOTS
            if cc >= XSLOTS:
                wait("sync", sa, prev_xq.get(cc - XSLOTS, 0))
                wait("sync", sv, prev_asm.get(cc - XSLOTS, 0))
            xsrc = bass.AP(x_d, (r * SEQ + jj * 128) * D, [[D, 128], [1, D]])
            m_xld[cc] = dma("sync", sxc[xslot], x_at[:, xslot, :], xsrc)

    def attn_pre_gen(sc, chunks=None):
        chunks = list(range(CPS) if chunks is None else chunks)
        ccs = [sc * CPS + c for c in chunks]
        v1 = {}
        mq = {}
        vc = {}
        for c, cc in zip(chunks, ccs):
            xslot = cc % XSLOTS
            wait("vector", sxc[xslot], m_xld[cc])
            v1[cc] = run("vector", lambda e, xslot=xslot, cc=cc:
                         e.tensor_reduce(
                col(MUR, cc), x_at[:, xslot, :], AX, ALU.add), sv)
        yield
        for c, cc in zip(chunks, ccs):
            xslot = cc % XSLOTS
            wait("scalar", sxc[xslot], m_xld[cc])
            wait("scalar", sa, junk_rel[0])
            if cc >= CPS:
                wait("scalar", stt, tr_mark.get(cc - CPS, 0))
            m_q = run("scalar", lambda e, xslot=xslot, cc=cc, c=c:
                      e.activation(
                xnbf[:, c, :], x_at[:, xslot, :], AF.Square,
                accum_out=col(QQ, cc)), sa)
            junk_rel[0] = m_q
            prev_xq[cc] = m_q
            mq[cc] = m_q
        yield
        for c, cc in zip(chunks, ccs):
            wait("vector", sa, mq[cc])
            wait("vector", sv, v1[cc])
            v2a = run("vector", lambda e, cc=cc: e.scalar_tensor_tensor(
                col(T0, cc), col(MUR, cc), 1.0 / D, col(MUR, cc),
                ALU.mult, ALU.mult), sv)
            wait("vector", sv, v2a)
            v2b = run("vector", lambda e, cc=cc: e.tensor_sub(
                col(VARP, cc), col(QQ, cc), col(T0, cc)), sv)
            wait("vector", sv, v2b)
            vc[cc] = run("vector", lambda e, cc=cc: e.tensor_scalar(
                col(VEPS, cc), col(VARP, cc), 1.0 / D, EPS1,
                ALU.mult, ALU.add), sv)
        yield
        s2m = {}
        for c, cc in zip(chunks, ccs):
            wait("scalar", sv, vc[cc])
            if cc < CPS:
                wait("scalar", sv, cst[0])
            run("scalar", lambda e, cc=cc: e.activation(
                col(T1, cc), col(VEPS, cc), AF.Sqrt), sa)
            s2m[cc] = run("scalar", lambda e, cc=cc: e.activation(
                col(SVC, cc), col(VARP, cc), AF.Sqrt, scale=invD[:]), sa)
        yield
        v7 = {}
        for c, cc in zip(chunks, ccs):
            wait("vector", sa, s2m[cc])
            v3 = run("vector", lambda e, cc=cc: e.reciprocal(
                col(RR, cc), col(T1, cc)), sv)
            wait("vector", sv, v3)
            v4 = run("vector", lambda e, cc=cc: e.tensor_scalar(
                col(AC, cc), col(SVC, cc), col(RR, cc), EPS2,
                ALU.mult, ALU.add), sv)
            wait("vector", sv, v4)
            v5 = run("vector", lambda e, cc=cc: e.reciprocal(
                col(T0, cc), col(AC, cc)), sv)
            wait("vector", sv, v5)
            v6 = run("vector", lambda e, cc=cc: e.tensor_mul(
                col(C1, cc), col(RR, cc), col(T0, cc)), sv)
            wait("vector", sv, v6)
            v7[cc] = run("vector", lambda e, cc=cc: e.scalar_tensor_tensor(
                col(NB, cc), col(MUR, cc), -1.0 / D, col(C1, cc),
                ALU.mult, ALU.mult), sv)
        yield
        for c, cc in zip(chunks, ccs):
            xslot = cc % XSLOTS
            wait("vector", sv, v7[cc])
            wait("vector", sa, junk_rel[0])
            if cc >= CPS:
                wait("vector", stt, tr_mark.get(cc - CPS, 0))
            m_xn = run("vector", lambda e, xslot=xslot, cc=cc, c=c:
                       e.tensor_scalar(
                xnbf[:, c, :], x_at[:, xslot, :], col(C1, cc), col(NB, cc),
                ALU.mult, ALU.add), sv)
            prev_xn[cc] = m_xn
        yield

    def ln2_gen(sc, args):
        ccs = [sc * CPS + c for c in range(CPS)]
        w1 = {}
        q2 = {}
        tc = {}
        sq2 = {}
        for c, cc in zip(range(CPS), ccs):
            m_dxa, yfb = args[c]
            wait("vector", sv, m_dxa)
            w1[cc] = run("vector", lambda e, cc=cc, yfb=yfb: e.tensor_reduce(
                col(M2R, cc), yfb, AX, ALU.add), sv)
        yield
        for c, cc in zip(range(CPS), ccs):
            m_dxa, yfb = args[c]
            wait("scalar", sv, m_dxa)
            wait("scalar", sa, junk_rel[0])
            wait("scalar", stt, tr_mark[cc])
            q2[cc] = run("scalar", lambda e, cc=cc, c=c, yfb=yfb:
                         e.activation(
                xnbf[:, c, :], yfb, AF.Square, accum_out=col(Q2, cc)), sa)
            junk_rel[0] = q2[cc]
        yield
        for c, cc in zip(range(CPS), ccs):
            wait("vector", sa, q2[cc])
            wait("vector", sv, w1[cc])
            t2a = run("vector", lambda e, cc=cc: e.scalar_tensor_tensor(
                col(T2, cc), col(M2R, cc), 1.0 / D, col(M2R, cc),
                ALU.mult, ALU.mult), sv)
            wait("vector", sv, t2a)
            t2b = run("vector", lambda e, cc=cc: e.tensor_sub(
                col(V2P, cc), col(Q2, cc), col(T2, cc)), sv)
            wait("vector", sv, t2b)
            tc[cc] = run("vector", lambda e, cc=cc: e.tensor_scalar(
                col(V2E, cc), col(V2P, cc), 1.0 / D, EPS1,
                ALU.mult, ALU.add), sv)
        yield
        for c, cc in zip(range(CPS), ccs):
            wait("scalar", sv, tc[cc])
            sq2[cc] = run("scalar", lambda e, cc=cc: e.activation(
                col(T2, cc), col(V2E, cc), AF.Sqrt), sa)
        yield
        for c, cc in zip(range(CPS), ccs):
            m_dxa, yfb = args[c]
            wait("vector", sa, sq2[cc])
            r2m = run("vector", lambda e, cc=cc: e.reciprocal(
                col(R2, cc), col(T2, cc)), sv)
            wait("vector", sv, r2m)
            n2m = run("vector", lambda e, cc=cc: e.scalar_tensor_tensor(
                col(N2, cc), col(M2R, cc), -1.0 / D, col(R2, cc),
                ALU.mult, ALU.mult), sv)
            wait("vector", sv, n2m)
            m_l2 = run("vector", lambda e, c=c, cc=cc, yfb=yfb:
                       e.tensor_scalar(
                ln2_sb[:, c, :], yfb, col(R2, cc), col(N2, cc),
                ALU.mult, ALU.add), sv)
            m_l2m[cc] = m_l2
        yield

    stepq = []

    def flush_steps(n):
        done = 0
        while stepq and done < n:
            try:
                next(stepq[0])
                done += 1
            except StopIteration:
                stepq.pop(0)

    def flush_all():
        while stepq:
            flush_steps(1000)

    with ExitStack() as pro:
        psb = lambda name, shape, dt: pro.enter_context(
            nc.sbuf_tensor(name, shape, dt))
        onesf = psb("onesf_s", [128, 128], F32)
        idf = psb("idf_s", [128, 128], F32)
        Asb = psb("Asb", [128, 128], F32)
        Esb = psb("Esb", [128, 128], F32)
        EsbT = psb("EsbT", [128, 128], F32)
        Ebc = psb("Ebc", [128, 2, 1024], F32)
        Msb = psb("Msb", [128, 1024], F32)
        g_sb = psb("g_sb", [128, DS], F32)
        gbf = psb("gbf", [128, DS], BF16)
        gT = psb("gT", [16, 128], BF16)
        zrow = psb("zrow", [16, 128], BF16)
        hank = psb("hank", [128, DS, 256], BF16)
        Bbf = psb("Bbf", [16, D], BF16)
        Csb = psb("Csb", [128, NDT, DS], F32)
        Cbf = psb("Cbf", [128, NDT, DS], BF16)
        p1a = psb("p1a", [128, 2048], F32)
        p1b = psb("p1b", [128, 2048], F32)
        p2a = psb("p2a", [128, 2048], F32)
        p2b = psb("p2b", [128, 2048], F32)
        p2c = psb("p2c", [128, 2048], F32)
        t1bf = psb("t1bf", [128, 2048], BF16)
        wtbf = psb("wtbf", [128, 4096], BF16)
        wstg = psb("wstg", [128, 2, 1024], BF16)
        trB = pro.enter_context(nc.psum_tensor("trB", [128, 1024], BF16))

        Asb3 = Asb[:].rearrange("p (a b) -> p a b", a=NDT)
        for hq in range(2):
            asrc = bass.AP(alog_d, hq * 4 * 128 * DS,
                           [[DS, 128], [128 * DS, 4], [1, DS]])
            dma("sync", s_sm, Asb3[:, hq * 4:(hq + 1) * 4, :], asrc)
        for hq in range(2):
            csrc = bass.AP(cw_d, hq * 4 * 128 * DS,
                           [[DS, 128], [128 * DS, 4], [1, DS]])
            dma("sync", s_sm, Csb[:, hq * 4:(hq + 1) * 4, :], csrc)
        dma("sync", s_sm, Dbc[:], bass.AP(dv_d, 0, [[0, 128], [1, D]]))
        dma("sync", s_sm, idb[:], idb_d.ap())
        dma("sync", s_sm, idf[:], idf_d.ap())
        dma("sync", s_sm, onesb[:], onesb_d.ap())
        dma("sync", s_sm, nkrev[:], nkrev_d.ap())
        SM = s_sm.n
        dma("sync", s_on, onesf[:], onesf_d.ap())
        dma("gpsimd", sg, gbc[:], bass.AP(ga_d, 0, [[0, 128], [1, D]]))
        dma("gpsimd", sg, Bbf[:], bw_d.ap())
        m_be = dma("gpsimd", sg, bbc[:], bass.AP(be_d, 0, [[0, 128], [1, D]]))

        emit_x_loads(0)
        stepq.append(attn_pre_gen(0))

        run("vector", lambda e: e.memset(invD[:], 1.0 / D), sv)
        run("vector", lambda e: e.memset(onef[:], 1.0), sv)
        cst[0] = run("vector", lambda e: e.memset(invDFF[:], 1.0 / DFF), sv)
        wait("vector", s_sm, SM)
        m_cbf = run("vector", lambda e: e.tensor_copy(Cbf[:], Csb[:]), sv)
        run("vector", lambda e: e.memset(bxs[:], 0.0), sv)

        wait("scalar", s_sm, SM)
        m_exp = run("scalar", lambda e: e.activation(Esb[:], Asb[:], AF.Exp), sa)
        wait("tensor", s_sm, SM)
        wait("tensor", sa, m_exp)
        m_etr = run("tensor",
                    lambda e: e.transpose(misc1[:, 0:128], Esb[:], idf[:]), stt)
        wait("vector", stt, m_etr)
        m_etre = run("vector", lambda e: e.tensor_copy(EsbT[:], misc1[:, 0:128]),
                     sv)
        wait("scalar", sv, m_etre)
        for dt in range(NDT):
            edst = bass.AP(erow_d, dt * 128, [[D, DS], [1, 128]])
            m_er = dma("scalar", sw, edst, EsbT[dt * 16:(dt + 1) * 16, :])

        wait("tensor", sv, m_cbf)
        for dt in range(NDT):
            m_bctr = run("tensor", lambda e, dt=dt: e.transpose(
                trA[0:16, dt * 128:(dt + 1) * 128], Cbf[:, dt, :], idb[:]), stt)
        wait("vector", stt, m_bctr)
        m_cT = run("vector", lambda e: e.tensor_copy(cT[:], trA[0:16, :]), sv)
        wait("tensor", sv, m_cT)
        wait("tensor", sg, m_be)
        for dt in range(NDT):
            m_btr = run("tensor", lambda e, dt=dt: e.transpose(
                trA[0:128, dt * 16:(dt + 1) * 16],
                Bbf[:, dt * 128:(dt + 1) * 128], idb[0:16, 0:16]), stt)
        wait("vector", stt, m_btr)
        m_bT0 = run("vector", lambda e: e.tensor_copy(
            bT[:].rearrange("p a b -> p (a b)"), trA[0:128, 0:128]), sv)

        pool = [p1a, p1b, p2a, p2b, p2c]
        psem = [sp1[0], sp1[1], sp2[0], sp2[1], sp2[2]]
        buf_rel = {}

        def pool_load(b, src, rearr):
            if b in buf_rel:
                wait("sync", *buf_rel[b])
            dst = (pool[b][:].rearrange("p (a b) -> p a b", a=2) if rearr
                   else pool[b][:])
            return dma("sync", psem[b], dst, src)

        m_absA = []
        for i in range(16):
            src = bass.AP(wup_d, i * 2 * 128 * D,
                          [[D, 128], [128 * D, 2], [1, D]])
            b = i % 5
            mld = pool_load(b, src, True)
            wait("vector", psem[b], mld)
            m_ab = run("vector", lambda e, buf=pool[b], i=i: e.tensor_reduce(
                acw[:, i:i + 1], buf[:], AX, ALU.add,
                apply_absolute_value=True), sv)
            m_absA.append(m_ab)
            buf_rel[b] = (sv, m_ab)
            flush_steps(1)

        wait("vector", sv, m_absA[15])
        m_ru = run("vector", lambda e: e.tensor_reduce(
            acw[:, 32:33], acw[:, 0:16], AX, ALU.add), sv)
        wait("tensor", s_on, 16)
        wait("tensor", sv, m_ru)
        m_ammu = run("tensor", lambda e: e.matmul(
            misc1[:, 0:1], onesf[:], acw[:, 32:33], start=True, stop=True), stt)
        wait("vector", stt, m_ammu)
        m_tau_u = run("vector", lambda e: e.tensor_scalar(
            tau_u[:], misc1[:, 0:1], 0.5 / (DFF * D), None, ALU.mult), sv)

        banks = [trA, trB]
        bank_rel = [0, 0]
        reg_rel = [0, 0]
        upar = [0]
        stg_dma = [0, 0]
        stg_slot = [wstg[:, 0, :], wstg[:, 1, :]]
        last_t = [0]

        def tern2048(buf, tau, m_tau_w, rpar, dsts):
            wait("vector", stt, reg_rel[rpar])
            wait("vector", sv, last_t[0])
            if m_tau_w:
                wait("vector", sv, m_tau_w)
            m_t1 = run("vector", lambda e, buf=buf, tau=tau: e.tensor_scalar(
                t1bf[:], buf[:], -1.0, tau[:], ALU.mult, ALU.is_gt), sv)
            wait("vector", sv, m_t1)
            m_t = run("vector", lambda e, buf=buf, tau=tau, rpar=rpar:
                      e.scalar_tensor_tensor(
                          wtbf[:, rpar * 2048:(rpar + 1) * 2048], buf[:],
                          tau[:], t1bf[:], ALU.is_gt, ALU.subtract), sv)
            m_trx = None
            for half in range(2):
                par = upar[0] % 2
                upar[0] += 1
                bank = banks[par]
                wait("tensor", sv, m_t)
                wait("tensor", sa, bank_rel[par])
                for k in range(8):
                    off = rpar * 2048 + half * 1024 + k * 128
                    m_trx = run("tensor", lambda e, bank=bank, k=k, off=off:
                                e.transpose(bank[:, k * 128:(k + 1) * 128],
                                            wtbf[:, off:off + 128],
                                            idb[:]), stt)
                wait("scalar", stt, m_trx)
                bank_rel[par] = dsts[half](bank)
            reg_rel[rpar] = m_trx
            last_t[0] = m_t
            return m_t

        def up_evac(fg):
            def f(bank, fg=fg):
                ss = fg % 2
                wait("scalar", swp[ss], stg_dma[ss])
                m_ev = run("scalar", lambda e, bank=bank, ss=ss: e.activation(
                    stg_slot[ss], bank[:], AF.Copy), sa)
                wait("scalar", sa, m_ev)
                wdst = bass.AP(wup_scr,
                               (fg // 2) * 128 * 2048 + (fg % 2) * 1024,
                               [[2048, 128], [128, NDT], [1, 128]])
                stg_dma[ss] = dma("scalar", swp[ss], wdst,
                                  stg_slot[ss].rearrange(
                                      "p (a b) -> p a b", a=NDT))
                return m_ev
            return f

        def dn_evac(j, ft0):
            def f(bank, j=j, ft0=ft0):
                return run("scalar", lambda e, bank=bank: e.activation(
                    WdnT[:, ft0:ft0 + 8, j * 128:(j + 1) * 128],
                    bank[:].rearrange("p (a b) -> p a b", a=8), AF.Copy), sa)
            return f

        m_abs = []
        for i in range(16):
            srcd = bass.AP(wdn_d, (i // 2) * 128 * DFF + (i % 2) * 2048,
                           [[DFF, 128], [1, 2048]])
            bd = i % 2
            mldd = pool_load(bd, srcd, False)
            srcu = bass.AP(wup_d, i * 2 * 128 * D,
                           [[D, 128], [128 * D, 2], [1, D]])
            bu = 2 + i % 3
            mldu = pool_load(bu, srcu, True)
            wait("scalar", psem[bd], mldd)
            if m_abs:
                wait("scalar", sa, m_abs[-1])
            m_ab = run("scalar", lambda e, bufd=pool[bd], i=i: e.activation(
                Ebc[:].rearrange("p a b -> p (a b)"), bufd[:], AF.Abs,
                accum_out=acw[:, 16 + i:17 + i]), sa)
            m_abs.append(m_ab)
            buf_rel[bd] = (sa, m_ab)
            wait("vector", psem[bu], mldu)
            m_t = tern2048(
                pool[bu], tau_u, m_tau_u if i == 0 else 0, i % 2,
                [up_evac(2 * i), up_evac(2 * i + 1)])
            buf_rel[bu] = (sv, m_t)
            flush_steps(1)

        wait("vector", sa, m_abs[15])
        m_rd = run("vector", lambda e: e.tensor_reduce(
            acw[:, 33:34], acw[:, 16:32], AX, ALU.add), sv)
        wait("tensor", sv, m_rd)
        m_ammd = run("tensor", lambda e: e.matmul(
            misc1[:, 1:2], onesf[:], acw[:, 33:34], start=True, stop=True), stt)
        wait("vector", stt, m_ammd)
        m_tau_d = run("vector", lambda e: e.tensor_scalar(
            tau_d[:], misc1[:, 1:2], 0.5 / (DFF * D), None, ALU.mult), sv)

        wait("scalar", sw, m_er)
        m_sl = []
        m_gexp = []
        for i in range(16):
            j, fh = i // 2, i % 2
            src = bass.AP(wdn_d, j * 128 * DFF + fh * 2048,
                          [[DFF, 128], [1, 2048]])
            b = i % 5
            mld = pool_load(b, src, False)
            wait("vector", psem[b], mld)
            m_t = tern2048(
                pool[b], tau_d, m_tau_d if i == 0 else 0, i % 2,
                [dn_evac(j, fh * 16), dn_evac(j, fh * 16 + 8)])
            buf_rel[b] = (sv, m_t)
            if i < 2:
                wait("scalar", sa, m_abs[15])
            else:
                wait("scalar", sa, m_gexp[i - 2])
            esrc = bass.AP(erow_d, i * 1024, [[0, 128], [1, 1024]])
            m_sl.append(dma("scalar", sgE[i % 2], Ebc[:, i % 2, :], esrc))
            wait("scalar", sgE[i % 2], m_sl[i])
            if m_gexp:
                wait("scalar", sa, m_gexp[-1])
            m_ge = run("scalar", lambda e, na=i: e.activation(
                Msb[:], Ebc[:, na % 2, :], AF.Exp, scale=nkrev[:],
                accum_out=g_sb[:, na:na + 1]), sa)
            m_gexp.append(m_ge)
            flush_steps(1)
        flush_all()

        wait("vector", sa, m_gexp[-1])
        m_gbf = run("vector", lambda e: e.tensor_scalar(
            gbf[:], g_sb[:], 1.0 / D, None, ALU.mult), sv)
        m_z = run("vector", lambda e: e.memset(zrow[:], 0.0), sv)
        wait("tensor", sv, m_gbf)
        wait("tensor", sa, bank_rel[0])
        m_gtr = run("tensor", lambda e: e.transpose(
            trA[0:16, 0:128], gbf[:], idb[:]), stt)
        wait("vector", stt, m_gtr)
        m_gT = run("vector", lambda e: e.tensor_copy(gT[:], trA[0:16, 0:128]),
                   sv)
        wait("scalar", sv, m_gT)
        dma("scalar", sw, bass.AP(gp_d, 128, [[384, DS], [1, 128]]), gT[:])
        dma("scalar", sw, bass.AP(gp_d, 0, [[384, DS], [1, 128]]), zrow[:])
        m_gw = dma("scalar", sw, bass.AP(gp_d, 256, [[384, DS], [1, 128]]),
                   zrow[:])
        wait("scalar", sw, m_gw)
        for hq in range(4):
            hsrc = bass.AP(gp_d, hq * 4 * 384, [[1, 128], [384, 4], [1, 256]])
            m_hank = dma("scalar", sw, hank[:, hq * 4:(hq + 1) * 4, :], hsrc)
        wait("vector", sw, m_hank)
        m_tall = run("vector", lambda e: e.tensor_copy(
            T_all[:], hank[:, :, ::-1]), sv)

    PRO_V, PRO_A, PRO_T, PRO_W = sv.n, sa.n, stt.n, sw.n
    PRO_SWP = [swp[0].n, swp[1].n]
    PRO_G = sg.n
    for eng in ("sync", "scalar", "vector", "tensor"):
        wait(eng, sv, PRO_V)
        wait(eng, sa, PRO_A)
        wait(eng, stt, PRO_T)
        wait(eng, sw, PRO_W)
        wait(eng, swp[0], PRO_SWP[0])
        wait(eng, swp[1], PRO_SWP[1])
        wait(eng, sg, PRO_G)
        wait(eng, sgE[0], sgE[0].n)
        wait(eng, sgE[1], sgE[1].n)
        wait(eng, s_sm, SM)
        wait(eng, s_on, 16)

    xnT = sb("xnT", [128, NDT, CPS * 128], BF16)
    ln2_sb = sb("ln2", [128, CPS, D], BF16)
    h_sb = sb("h", [128, NFT, CPS * 128], BF16)
    out_sb = sb("out_sb", [128, 2, D], F32)
    tmpA = sb("tmpA", [128, D], F32)
    yf_sb = sb("yf", [128, CPS, D], F32)
    h2_sb = sb("h2", [128, 3, CPS * 128], BF16)
    hm_bf = sb("hm_bf", [128, CPS, DS], BF16)
    hmT = sb("hmT", [16, CPS * 128], BF16)
    qrow = sb("qrow", [1, 512], F32)
    mm1a = pa("mm1a", [128, 512], F32)
    mm1b = pa("mm1b", [128, 512], F32)
    o2set = [[pa("o2Alo", [128, 512], F32), pa("o2Ahi", [128, 512], F32)],
             [pa("o2Blo", [128, 512], F32), pa("o2Bhi", [128, 512], F32)]]
    trA_rel = sv.n

    for sc in range(NSC if STAGE >= 2 else 0):
        row = (sc * CPS) // 16
        j0 = (sc * CPS) % 16
        slot0 = row * 17 + 1 + j0

        m_xnT_ev = []
        m_bxe = []
        for c in range(CPS):
            cc = sc * CPS + c
            wait("tensor", sv, prev_xn[cc])
            wait("tensor", sv, trA_rel)
            m_tr = None
            for dt in range(NDT):
                m_tr = run("tensor", lambda e, dt=dt, c=c: e.transpose(
                    trA[:, dt * 128:(dt + 1) * 128],
                    xnbf[:, c, dt * 128:(dt + 1) * 128], idb[:]), stt)
            tr_mark[cc] = m_tr
            wait("vector", stt, m_tr)
            if c == 0:
                wait("vector", stt, prev_mm1_done)
            m_ev = run("vector", lambda e, c=c: e.tensor_copy(
                xnT[:, :, c * 128:(c + 1) * 128],
                trA[:].rearrange("p (a b) -> p a b", a=NDT)), sv)
            trA_rel = m_ev
            m_xnT_ev.append(m_ev)
            wait("tensor", sv, m_ev)
            if c == 0 and misc1_rel[0] is not None:
                wait("tensor", *misc1_rel)
            m_bx = None
            for dt in range(NDT):
                m_bx = run("tensor", lambda e, dt=dt, c=c: e.matmul(
                    misc1[:, c * 16:(c + 1) * 16],
                    xnT[:, dt, c * 128:(c + 1) * 128],
                    bT[:, dt, :], start=(dt == 0), stop=(dt == NDT - 1)), stt)
            wait("vector", stt, m_bx)
            slot = row * 17 + 1 + (cc % 16)
            m_bxev = run("vector", lambda e, c=c, cc=cc, slot=slot:
                         e.tensor_scalar(
                bxs[:, slot, :], misc1[:, c * 16:(c + 1) * 16],
                col(AC, cc), None, ALU.mult), sv)
            m_bxe.append(m_bxev)

        wait("tensor", sv, m_bxe[-1])
        m_cv = None
        for n in range(DS):
            run("tensor", lambda e, n=n: e.matmul(
                misc1[:, 256 + n * CPS:256 + (n + 1) * CPS],
                T_all[:, n, 0:128], bxs[:, slot0:slot0 + CPS, n],
                start=True, stop=False), None)
            m_cv = run("tensor", lambda e, n=n: e.matmul(
                misc1[:, 256 + n * CPS:256 + (n + 1) * CPS],
                T_all[:, n, 128:256], bxs[:, slot0 - 1:slot0 - 1 + CPS, n],
                start=False, stop=True), stt)
        wait("vector", stt, m_cv)
        m_hm = run("vector", lambda e: e.tensor_copy(
            hm_bf[:], misc1[:, 256:256 + DS * CPS].rearrange(
                "p (n c) -> p c n", n=DS)), sv)
        misc1_rel = (sv, m_hm)

        m_hte = 0
        ln2_args = []
        for c in range(CPS):
            cc = sc * CPS + c
            xslot = cc % XSLOTS
            oset = o2set[cc % 2]
            yfb = yf_sb[:, c, :]
            wait("tensor", sv, m_hm if c == 0 else m_hte)
            wait("tensor", sv, trA_rel)
            m_ht = run("tensor", lambda e, c=c: e.transpose(
                trA[0:16, c * 128:(c + 1) * 128], hm_bf[:, c, :], idb[:]), stt)
            wait("vector", stt, m_ht)
            m_hte = run("vector", lambda e, c=c: e.tensor_copy(
                hmT[:, c * 128:(c + 1) * 128],
                trA[0:16, c * 128:(c + 1) * 128]), sv)
            trA_rel = m_hte
            m_dxa = 0
            for hf in range(2):
                wait("tensor", sv, m_hte if hf == 0 else m_dxa)
                if hf == 0:
                    wait("tensor", sv, o2_rel[cc % 2])
                m_y = run("tensor", lambda e, c=c, hf=hf, oset=oset: e.matmul(
                    oset[hf][:], hmT[:, c * 128:(c + 1) * 128],
                    cT[:, hf * 512:(hf + 1) * 512], start=True, stop=True), stt)
                if c == 0 and hf == 0:
                    wait("vector", sv, last_asm_mark)
                dxm = run("vector", lambda e, xslot=xslot, hf=hf: e.tensor_mul(
                    tmpA[:, hf * 512:(hf + 1) * 512],
                    x_at[:, xslot, hf * 512:(hf + 1) * 512],
                    Dbc[:, hf * 512:(hf + 1) * 512]), sv)
                wait("vector", stt, m_y)
                wait("vector", sv, dxm)
                if hf == 0 and cc >= CPS:
                    wait("vector", sv, m_l2m.get(cc - CPS, 0))
                m_dxa = run("vector", lambda e, hf=hf, oset=oset, yfb=yfb:
                            e.tensor_add(
                    yfb[:, hf * 512:(hf + 1) * 512], oset[hf][:],
                    tmpA[:, hf * 512:(hf + 1) * 512]), sv)
            prev_dx[cc] = m_dxa
            ln2_args.append((m_dxa, yfb))

        stepq.append(ln2_gen(sc, ln2_args))
        if sc + 1 < NSC:
            emit_x_loads(sc + 1, range(2))
            stepq.append(attn_pre_gen(sc + 1, range(2)))

        if STAGE < 2:
            continue
        mm1banks = [mm1a, mm1b]
        m_h2 = [None] * NFT
        m_qmm = [0] * NFT
        wait("tensor", sv, m_xnT_ev[-1])
        if sc == 1:
            wait("tensor", *misc1_rel)

        def issue_wup(sup):
            slot = sup % 2
            wait("scalar", stt, wup_rel[slot])
            wsrc = bass.AP(wup_scr, sup * 128 * 2048,
                           [[2048, 128], [1, 2048]])
            wup_ld[sup] = dma("scalar", swp[slot],
                              wupb[:, slot, :, :, :].rearrange(
                                  "p a b c -> p (a b c)"), wsrc)

        def emit_qmm(fg):
            wait("tensor", sv, m_h2[fg])
            if fg == 0:
                wait("tensor", *misc1_rel)
            m_qmm[fg] = run("tensor", lambda e, fg=fg: e.matmul(
                misc1[0:1, 0:512], onesb[:], h2_sb[:, fg % 3, :],
                start=(fg == 0), stop=(fg == NFT - 1),
                skip_group_check=True), stt)

        issue_wup(0)
        issue_wup(1)
        for fg in range(NFT):
            sup, fi = fg // 2, fg % 2
            bank = mm1banks[fg % 2]
            wait("tensor", sa, mm1_rel[fg % 2])
            if fi == 0:
                wait("tensor", swp[sup % 2], wup_ld[sup])
            m_m1 = None
            for dt in range(NDT):
                m_m1 = run("tensor", lambda e, bank=bank, dt=dt, sup=sup,
                           fi=fi: e.matmul(
                    bank[:, 0:CPS * 128], wupb[:, sup % 2, fi, dt, :],
                    xnT[:, dt, :], start=(dt == 0), stop=(dt == NDT - 1)), stt)
            if fi == 1:
                wup_rel[sup % 2] = m_m1
            if fg == NFT - 1:
                prev_mm1_done = m_m1
            if fg >= 2:
                emit_qmm(fg - 2)
            wait("scalar", stt, m_m1)
            if fi == 1 and sup + 2 < NFT // 2:
                issue_wup(sup + 2)
            if sc > 0 and fg < 2:
                wait("scalar", stt, prev_mm2_done)
            m_g1 = run("scalar", lambda e, bank=bank, fg=fg: e.activation(
                h_sb[:, fg, :], bank[:, 0:CPS * 128], AF.Gelu), sa)
            mm1_rel[fg % 2] = m_g1
            wait("vector", sa, m_g1)
            if fg >= 3:
                wait("vector", stt, m_qmm[fg - 3])
            elif sc > 0:
                wait("vector", stt, prev_q_last)
            m_h2[fg] = run("vector", lambda e, fg=fg: e.tensor_mul(
                h2_sb[:, fg % 3, :], h_sb[:, fg, :], h_sb[:, fg, :]), sv)
            flush_steps(2)
        emit_qmm(NFT - 2)
        emit_qmm(NFT - 1)
        prev_q_last = m_qmm[NFT - 1]
        flush_all()

        QS, SQ = 18, 19
        m_sce = 0
        wait("vector", stt, m_qmm[NFT - 1])
        m_qrow = run("vector", lambda e: e.tensor_copy(
            qrow[:], misc1[0:1, 0:512]), sv)
        wait("tensor", sv, m_qrow)
        sqm = 0
        for c in range(CPS):
            m_qtr = run("tensor", lambda e, c=c: e.transpose(
                misc1[:, 320 + c:321 + c], qrow[0:1, c * 128:(c + 1) * 128],
                onef[:]), stt)
            wait("scalar", stt, m_qtr)
            sqm = run("scalar", lambda e, c=c: e.activation(
                stc[:, SQ + 24 * c:SQ + 24 * c + 1],
                misc1[:, 320 + c:321 + c], AF.Sqrt,
                scale=invDFF[:]), sa)
            wait("vector", sa, sqm)
            d1 = run("vector", lambda e, c=c: e.tensor_scalar(
                stc[:, QS + 24 * c:QS + 24 * c + 1],
                stc[:, SQ + 24 * c:SQ + 24 * c + 1], EPS2, None, ALU.add), sv)
            wait("vector", sv, d1)
            m_sce = run("vector", lambda e, c=c: e.reciprocal(
                scol[:, c:c + 1], stc[:, QS + 24 * c:QS + 24 * c + 1]), sv)
        misc1_rel = (sa, sqm)

        for c in range(CPS):
            cc = sc * CPS + c
            r, jj = cc // 16, cc % 16
            xslot = cc % XSLOTS
            ab = cc % 2
            oset = o2set[cc % 2]
            m_m2l = m_m2h = None
            wait("tensor", sv, o2_rel[cc % 2])
            wait("tensor", sv, prev_dx[cc])
            for fg in range(NFT):
                m_m2l = run("tensor", lambda e, c=c, fg=fg, oset=oset: e.matmul(
                    oset[0][:], h_sb[:, fg, c * 128:(c + 1) * 128],
                    WdnT[:, fg, 0:512],
                    start=(fg == 0), stop=(fg == NFT - 1),
                    skip_group_check=True), stt)
                m_m2h = run("tensor", lambda e, c=c, fg=fg, oset=oset: e.matmul(
                    oset[1][:], h_sb[:, fg, c * 128:(c + 1) * 128],
                    WdnT[:, fg, 512:1024],
                    start=(fg == 0), stop=(fg == NFT - 1),
                    skip_group_check=True), stt)
            if c == CPS - 1:
                prev_mm2_done = m_m2h
            if sc == 0 and c == 0:
                wait("vector", sg, m_be)
            wait("vector", sv, last_asm_mark)
            wait("vector", sv, m_l2m[cc])
            p1 = run("vector", lambda e, c=c: e.tensor_mul(
                tmpA[:], ln2_sb[:, c, :], gbc[:]), sv)
            wait("vector", sv, p1)
            p2 = run("vector", lambda e: e.tensor_add(tmpA[:], tmpA[:], bbc[:]),
                sv)
            wait("vector", sv, p2)
            p3 = run("vector", lambda e, xslot=xslot: e.tensor_add(
                tmpA[:], tmpA[:], x_at[:, xslot, :]), sv)
            wait("vector", sv, p3)
            if cc >= 2:
                wait("vector", soo[ab], m_out.get(cc - 2, 0))
            wait("vector", stt, m_m2h)
            run("vector", lambda e, c=c, ab=ab, oset=oset: e.scalar_tensor_tensor(
                out_sb[:, ab, 0:512], oset[0][:], scol[:, c:c + 1],
                tmpA[:, 0:512], ALU.mult, ALU.add), None)
            m_asm = run("vector", lambda e, c=c, ab=ab, oset=oset: e.scalar_tensor_tensor(
                out_sb[:, ab, 512:1024], oset[1][:], scol[:, c:c + 1],
                tmpA[:, 512:1024], ALU.mult, ALU.add), sv)
            o2_rel[cc % 2] = m_asm
            prev_asm[cc] = m_asm
            last_asm_mark = m_asm
            wait("sync", sv, m_asm)
            ydst = bass.AP(y_d, (r * SEQ + jj * 128) * D, [[D, 128], [1, D]])
            m_o = dma("sync", soo[ab], ydst, out_sb[:, ab, :])
            m_out[cc] = m_o

        if sc + 1 < NSC:
            emit_x_loads(sc + 1, [2, 3])
            stepq.append(attn_pre_gen(sc + 1, [2, 3]))
            flush_all()

    if STAGE >= 3:
        wait("sync", soo[0], soo[0].n)
        wait("sync", soo[1], soo[1].n)
    else:
        wait("sync", sv, sv.n)
        wait("sync", sa, sa.n)
        wait("sync", stt, stt.n)

    with nc.Block() as block:
        @block.sync
        def _(e):
            for f in ops["sync"]:
                f(e)

        @block.scalar
        def _(e):
            for f in ops["scalar"]:
                f(e)

        @block.vector
        def _(e):
            for f in ops["vector"]:
                f(e)

        @block.tensor
        def _(e):
            for f in ops["tensor"]:
                f(e)

        @block.gpsimd
        def _(e):
            for f in ops["gpsimd"]:
                f(e)

    return nc


_NC = None


def kernel(L0_attn_x, L0_attn_A_log, L0_attn_B_w, L0_attn_C_w, L0_attn_D,
           L0_attn_gamma, L0_attn_beta, L0_ffn_w_up, L0_ffn_w_down):
    global _NC
    f32 = np.float32
    x = np.ascontiguousarray(np.asarray(L0_attn_x, f32))
    wup = np.ascontiguousarray(np.asarray(L0_ffn_w_up, f32))
    wdn = np.ascontiguousarray(np.asarray(L0_ffn_w_down, f32))
    alog = np.ascontiguousarray(np.asarray(L0_attn_A_log, f32))
    bw = np.ascontiguousarray(np.asarray(L0_attn_B_w, f32))
    cw = np.ascontiguousarray(np.asarray(L0_attn_C_w, f32))
    dv = np.ascontiguousarray(np.asarray(L0_attn_D, f32)).reshape(1, D)
    ga = np.ascontiguousarray(np.asarray(L0_attn_gamma, f32)).reshape(1, D)
    be = np.ascontiguousarray(np.asarray(L0_attn_beta, f32)).reshape(1, D)

    idb = np.eye(128, dtype=ml_dtypes.bfloat16)
    idf = np.eye(128, dtype=f32)
    onesb = np.ones((128, 1), dtype=ml_dtypes.bfloat16)
    onesf = np.ones((128, 128), dtype=f32)
    nkrev = (np.arange(128, dtype=f32) - 127.0).reshape(128, 1)

    if _NC is None:
        _NC = build_nc()

    in_maps = [{
        "x": x[2 * core:2 * core + 2],
        "wup": wup, "wdn": wdn, "alog": alog, "bw": bw, "cw": cw,
        "dv": dv, "ga": ga, "be": be,
        "idb": idb, "idf": idf, "onesb": onesb, "onesf": onesf,
        "nkrev": nkrev,
    } for core in range(8)]
    res = run_bass_kernel_spmd(_NC, in_maps, core_ids=list(range(8)))
    return np.concatenate([r["y"] for r in res.results], axis=0)


# version 16
# speedup vs baseline: 1.0896x; 1.0338x over previous
import numpy as np
import ml_dtypes
from contextlib import ExitStack
import concourse.bass as bass
import concourse.mybir as mybir
from concourse.bass_utils import run_bass_kernel_spmd

F32 = mybir.dt.float32
BF16 = mybir.dt.bfloat16
AF = mybir.ActivationFunctionType
ALU = mybir.AluOpType
AX = mybir.AxisListType.X

D = 1024
DS = 16
SEQ = 2048
NCH = 32
CPS = 4
NSC = NCH // CPS
DFF = 4096
NFT = 32
NDT = 8
EPS1 = 1e-5
EPS2 = 1e-8


class Ctr:
    def __init__(self, sem):
        self.sem = sem
        self.n = 0


RACE_DEBUG = False
DETECT = False
STAGE = 3


def build_nc():
    nc = bass.Bass("TRN2", target_bir_lowering=False,
                   detect_race_conditions=RACE_DEBUG or DETECT)

    x_d = nc.dram_tensor("x", [2, SEQ, D], F32, kind="ExternalInput")
    wup_d = nc.dram_tensor("wup", [DFF, D], F32, kind="ExternalInput")
    wdn_d = nc.dram_tensor("wdn", [D, DFF], F32, kind="ExternalInput")
    alog_d = nc.dram_tensor("alog", [D, DS], F32, kind="ExternalInput")
    bw_d = nc.dram_tensor("bw", [DS, D], F32, kind="ExternalInput")
    cw_d = nc.dram_tensor("cw", [D, DS], F32, kind="ExternalInput")
    dv_d = nc.dram_tensor("dv", [1, D], F32, kind="ExternalInput")
    ga_d = nc.dram_tensor("ga", [1, D], F32, kind="ExternalInput")
    be_d = nc.dram_tensor("be", [1, D], F32, kind="ExternalInput")
    idb_d = nc.dram_tensor("idb", [128, 128], BF16, kind="ExternalInput")
    idf_d = nc.dram_tensor("idf", [128, 128], F32, kind="ExternalInput")
    onesb_d = nc.dram_tensor("onesb", [128, 1], BF16, kind="ExternalInput")
    onesf_d = nc.dram_tensor("onesf", [128, 128], F32, kind="ExternalInput")
    nkrev_d = nc.dram_tensor("nkrev", [128, 1], F32, kind="ExternalInput")
    y_d = nc.dram_tensor("y", [2, SEQ, D], F32, kind="ExternalOutput")
    erow_d = nc.dram_tensor("erow_scr", [1, DS * D], F32, kind="Internal")
    gp_d = nc.dram_tensor("gp_scr", [DS, 384], BF16, kind="Internal")
    wup_scr = nc.dram_tensor("wup_scr", [NFT // 2, 128, 2, NDT, 128], BF16,
                             kind="Internal")

    sb = nc.alloc_sbuf_tensor
    wupb = sb("wupb", [128, 2, 2, NDT, 128], BF16)
    WdnT = sb("WdnT", [128, NFT, D], BF16)
    T_all = sb("T_all", [128, DS, 256], BF16)
    bxs = sb("bxs", [128, 34, DS], BF16)
    bT = sb("bT", [128, NDT, DS], BF16)
    cT = sb("cT", [16, NDT * 128], BF16)
    Dbc = sb("Dbc", [128, D], F32)
    gbc = sb("gbc", [128, D], BF16)
    bbc = sb("bbc", [128, D], BF16)
    idb = sb("idb_s", [128, 128], BF16)
    onesb = sb("onesb_s", [128, 1], BF16)
    nkrev = sb("nkrev_s", [128, 1], F32)
    onef = sb("onef", [1, 1], F32)
    tau_u = sb("tau_u", [128, 1], F32)
    tau_d = sb("tau_d", [128, 1], F32)
    acw = sb("acw", [128, 34], F32)


    pa = nc.alloc_psum_tensor
    trA = pa("trA", [128, 1024], BF16)
    misc1 = pa("misc1", [128, 512], F32)

    s_sm = Ctr(nc.alloc_semaphore("s_sm"))
    s_on = Ctr(nc.alloc_semaphore("s_on"))
    sw = Ctr(nc.alloc_semaphore("sw"))
    sp1 = [Ctr(nc.alloc_semaphore("sp1a")), Ctr(nc.alloc_semaphore("sp1b"))]
    sp2 = [Ctr(nc.alloc_semaphore("sp2a")), Ctr(nc.alloc_semaphore("sp2b")),
           Ctr(nc.alloc_semaphore("sp2c"))]
    sxc = [Ctr(nc.alloc_semaphore(f"sxc{i}")) for i in range(7)]
    soo = [Ctr(nc.alloc_semaphore("soa")), Ctr(nc.alloc_semaphore("sob"))]
    swp = [Ctr(nc.alloc_semaphore("swpa")), Ctr(nc.alloc_semaphore("swpb")),
           Ctr(nc.alloc_semaphore("swpc")), Ctr(nc.alloc_semaphore("swpd"))]
    sgE = [Ctr(nc.alloc_semaphore("sgE0")), Ctr(nc.alloc_semaphore("sgE1"))]
    sv = Ctr(nc.alloc_semaphore("sv"))
    sa = Ctr(nc.alloc_semaphore("sa"))
    stt = Ctr(nc.alloc_semaphore("stt"))
    sg = Ctr(nc.alloc_semaphore("sg"))

    ops = {k: [] for k in ("sync", "scalar", "vector", "tensor", "gpsimd")}

    def em(eng, fn):
        ops[eng].append(fn)

    def dma(eng, ctr, out_ap, in_ap):
        def f(e, o=out_ap, i=in_ap, s=ctr.sem):
            e.dma_start(out=o, in_=i).then_inc(s, 16)
        em(eng, f)
        ctr.n += 16
        return ctr.n

    def wait(eng, ctr, v):
        if v <= 0:
            return
        s = ctr.sem if isinstance(ctr, Ctr) else ctr

        def f(e, s=s, v=v):
            e.wait_ge(s, v)
        em(eng, f)

    eng_ctr = {"vector": sv, "scalar": sa, "tensor": stt}

    def run(eng, fn, inc=None):
        if RACE_DEBUG and eng in eng_ctr:
            c = eng_ctr[eng]
            if c.n:
                wait(eng, c, c.n)
            inc = c

        if inc is None:
            em(eng, lambda e, fn=fn: fn(e))
            return None

        def f(e, fn=fn, s=inc.sem):
            fn(e).then_inc(s, 1)
        em(eng, f)
        inc.n += 1
        return inc.n

    x_at = sb("x_at", [128, 6, D], F32)
    xnbf = sb("xnbf", [128, CPS, D], BF16)
    stc = sb("stc", [128, 24 * CPS], F32)
    scol = sb("scol", [128, CPS], F32)
    invD = sb("invD", [128, 1], F32)
    invDFF = sb("invDFF", [128, 1], F32)
    (MUR, QQ, VARP, VEPS, T1, SVC, RR, AC, T0, C1, NB,
     M2R, Q2, V2P, V2E, T2, R2, N2) = range(18)

    o2_rel = [0, 0]
    misc1_rel = (None, 0)
    mm1_rel = [0, 0]
    prev_mm1_done = 0
    prev_mm2_done = 0
    prev_asm = {}
    prev_xn = {}
    prev_xq = {}
    prev_dx = {}
    m_l2m = {}
    tr_mark = {}
    junk_rel = [0]
    prev_q_last = 0
    m_out = {}
    m_xld = {}
    last_asm_mark = 0
    wup_ld = {}
    wup_rel = [0, 0]
    cst = [0]
    XSLOTS = 6

    def col(k, cc):
        c = cc % CPS
        return stc[:, k + 24 * c:k + 24 * c + 1]

    def emit_x_loads(sc, chunks=None):
        for c in (range(CPS) if chunks is None else chunks):
            cc = sc * CPS + c
            r, jj = cc // 16, cc % 16
            xslot = cc % XSLOTS
            if cc >= XSLOTS:
                wait("sync", sa, prev_xq.get(cc - XSLOTS, 0))
                wait("sync", sv, prev_asm.get(cc - XSLOTS, 0))
            xsrc = bass.AP(x_d, (r * SEQ + jj * 128) * D, [[D, 128], [1, D]])
            m_xld[cc] = dma("sync", sxc[xslot], x_at[:, xslot, :], xsrc)

    def attn_pre_gen(sc, chunks=None):
        chunks = list(range(CPS) if chunks is None else chunks)
        ccs = [sc * CPS + c for c in chunks]
        v1 = {}
        mq = {}
        vc = {}
        for c, cc in zip(chunks, ccs):
            xslot = cc % XSLOTS
            wait("vector", sxc[xslot], m_xld[cc])
            v1[cc] = run("vector", lambda e, xslot=xslot, cc=cc:
                         e.tensor_reduce(
                col(MUR, cc), x_at[:, xslot, :], AX, ALU.add), sv)
        yield
        for c, cc in zip(chunks, ccs):
            xslot = cc % XSLOTS
            wait("scalar", sxc[xslot], m_xld[cc])
            wait("scalar", sa, junk_rel[0])
            if cc >= CPS:
                wait("scalar", stt, tr_mark.get(cc - CPS, 0))
            m_q = run("scalar", lambda e, xslot=xslot, cc=cc, c=c:
                      e.activation(
                xnbf[:, c, :], x_at[:, xslot, :], AF.Square,
                accum_out=col(QQ, cc)), sa)
            junk_rel[0] = m_q
            prev_xq[cc] = m_q
            mq[cc] = m_q
        yield
        for c, cc in zip(chunks, ccs):
            wait("vector", sa, mq[cc])
            wait("vector", sv, v1[cc])
            v2a = run("vector", lambda e, cc=cc: e.scalar_tensor_tensor(
                col(T0, cc), col(MUR, cc), 1.0 / D, col(MUR, cc),
                ALU.mult, ALU.mult), sv)
            wait("vector", sv, v2a)
            v2b = run("vector", lambda e, cc=cc: e.tensor_sub(
                col(VARP, cc), col(QQ, cc), col(T0, cc)), sv)
            wait("vector", sv, v2b)
            vc[cc] = run("vector", lambda e, cc=cc: e.tensor_scalar(
                col(VEPS, cc), col(VARP, cc), 1.0 / D, EPS1,
                ALU.mult, ALU.add), sv)
        yield
        s2m = {}
        for c, cc in zip(chunks, ccs):
            wait("scalar", sv, vc[cc])
            if cc < CPS:
                wait("scalar", sv, cst[0])
            run("scalar", lambda e, cc=cc: e.activation(
                col(T1, cc), col(VEPS, cc), AF.Sqrt), sa)
            s2m[cc] = run("scalar", lambda e, cc=cc: e.activation(
                col(SVC, cc), col(VARP, cc), AF.Sqrt, scale=invD[:]), sa)
        yield
        v7 = {}
        for c, cc in zip(chunks, ccs):
            wait("vector", sa, s2m[cc])
            v3 = run("vector", lambda e, cc=cc: e.reciprocal(
                col(RR, cc), col(T1, cc)), sv)
            wait("vector", sv, v3)
            v4 = run("vector", lambda e, cc=cc: e.tensor_scalar(
                col(AC, cc), col(SVC, cc), col(RR, cc), EPS2,
                ALU.mult, ALU.add), sv)
            wait("vector", sv, v4)
            v5 = run("vector", lambda e, cc=cc: e.reciprocal(
                col(T0, cc), col(AC, cc)), sv)
            wait("vector", sv, v5)
            v6 = run("vector", lambda e, cc=cc: e.tensor_mul(
                col(C1, cc), col(RR, cc), col(T0, cc)), sv)
            wait("vector", sv, v6)
            v7[cc] = run("vector", lambda e, cc=cc: e.scalar_tensor_tensor(
                col(NB, cc), col(MUR, cc), -1.0 / D, col(C1, cc),
                ALU.mult, ALU.mult), sv)
        yield
        for c, cc in zip(chunks, ccs):
            xslot = cc % XSLOTS
            wait("vector", sv, v7[cc])
            wait("vector", sa, junk_rel[0])
            if cc >= CPS:
                wait("vector", stt, tr_mark.get(cc - CPS, 0))
            m_xn = run("vector", lambda e, xslot=xslot, cc=cc, c=c:
                       e.tensor_scalar(
                xnbf[:, c, :], x_at[:, xslot, :], col(C1, cc), col(NB, cc),
                ALU.mult, ALU.add), sv)
            prev_xn[cc] = m_xn
        yield

    def ln2_gen(sc, args):
        ccs = [sc * CPS + c for c in range(CPS)]
        w1 = {}
        q2 = {}
        tc = {}
        sq2 = {}
        for c, cc in zip(range(CPS), ccs):
            m_dxa, yfb = args[c]
            wait("vector", sv, m_dxa)
            w1[cc] = run("vector", lambda e, cc=cc, yfb=yfb: e.tensor_reduce(
                col(M2R, cc), yfb, AX, ALU.add), sv)
        yield
        for c, cc in zip(range(CPS), ccs):
            m_dxa, yfb = args[c]
            wait("scalar", sv, m_dxa)
            wait("scalar", sa, junk_rel[0])
            wait("scalar", stt, tr_mark[cc])
            q2[cc] = run("scalar", lambda e, cc=cc, c=c, yfb=yfb:
                         e.activation(
                xnbf[:, c, :], yfb, AF.Square, accum_out=col(Q2, cc)), sa)
            junk_rel[0] = q2[cc]
        yield
        for c, cc in zip(range(CPS), ccs):
            wait("vector", sa, q2[cc])
            wait("vector", sv, w1[cc])
            t2a = run("vector", lambda e, cc=cc: e.scalar_tensor_tensor(
                col(T2, cc), col(M2R, cc), 1.0 / D, col(M2R, cc),
                ALU.mult, ALU.mult), sv)
            wait("vector", sv, t2a)
            t2b = run("vector", lambda e, cc=cc: e.tensor_sub(
                col(V2P, cc), col(Q2, cc), col(T2, cc)), sv)
            wait("vector", sv, t2b)
            tc[cc] = run("vector", lambda e, cc=cc: e.tensor_scalar(
                col(V2E, cc), col(V2P, cc), 1.0 / D, EPS1,
                ALU.mult, ALU.add), sv)
        yield
        for c, cc in zip(range(CPS), ccs):
            wait("scalar", sv, tc[cc])
            sq2[cc] = run("scalar", lambda e, cc=cc: e.activation(
                col(T2, cc), col(V2E, cc), AF.Sqrt), sa)
        yield
        for c, cc in zip(range(CPS), ccs):
            m_dxa, yfb = args[c]
            wait("vector", sa, sq2[cc])
            r2m = run("vector", lambda e, cc=cc: e.reciprocal(
                col(R2, cc), col(T2, cc)), sv)
            wait("vector", sv, r2m)
            n2m = run("vector", lambda e, cc=cc: e.scalar_tensor_tensor(
                col(N2, cc), col(M2R, cc), -1.0 / D, col(R2, cc),
                ALU.mult, ALU.mult), sv)
            wait("vector", sv, n2m)
            m_l2 = run("vector", lambda e, c=c, cc=cc, yfb=yfb:
                       e.tensor_scalar(
                ln2_sb[:, c, :], yfb, col(R2, cc), col(N2, cc),
                ALU.mult, ALU.add), sv)
            m_l2m[cc] = m_l2
        yield

    stepq = []

    def flush_steps(n):
        done = 0
        while stepq and done < n:
            try:
                next(stepq[0])
                done += 1
            except StopIteration:
                stepq.pop(0)

    def flush_all():
        while stepq:
            flush_steps(1000)

    with ExitStack() as pro:
        psb = lambda name, shape, dt: pro.enter_context(
            nc.sbuf_tensor(name, shape, dt))
        onesf = psb("onesf_s", [128, 128], F32)
        idf = psb("idf_s", [128, 128], F32)
        Asb = psb("Asb", [128, 128], F32)
        Esb = psb("Esb", [128, 128], F32)
        EsbT = psb("EsbT", [128, 128], F32)
        Ebc = psb("Ebc", [128, 2, 1024], F32)
        Msb = psb("Msb", [128, 1024], F32)
        g_sb = psb("g_sb", [128, DS], F32)
        gbf = psb("gbf", [128, DS], BF16)
        gT = psb("gT", [16, 128], BF16)
        zrow = psb("zrow", [16, 128], BF16)
        hank = psb("hank", [128, DS, 256], BF16)
        Bbf = psb("Bbf", [16, D], BF16)
        Csb = psb("Csb", [128, NDT, DS], F32)
        Cbf = psb("Cbf", [128, NDT, DS], BF16)
        p1a = psb("p1a", [128, 2048], F32)
        p1b = psb("p1b", [128, 2048], F32)
        p2a = psb("p2a", [128, 2048], F32)
        p2b = psb("p2b", [128, 2048], F32)
        p2c = psb("p2c", [128, 2048], F32)
        t1bf = psb("t1bf", [128, 2048], BF16)
        wtbf = psb("wtbf", [128, 4096], BF16)
        wstg = psb("wstg", [128, 2, 1024], BF16)
        trB = pro.enter_context(nc.psum_tensor("trB", [128, 1024], BF16))

        Asb3 = Asb[:].rearrange("p (a b) -> p a b", a=NDT)
        for hq in range(2):
            asrc = bass.AP(alog_d, hq * 4 * 128 * DS,
                           [[DS, 128], [128 * DS, 4], [1, DS]])
            dma("sync", s_sm, Asb3[:, hq * 4:(hq + 1) * 4, :], asrc)
        for hq in range(2):
            csrc = bass.AP(cw_d, hq * 4 * 128 * DS,
                           [[DS, 128], [128 * DS, 4], [1, DS]])
            dma("sync", s_sm, Csb[:, hq * 4:(hq + 1) * 4, :], csrc)
        dma("sync", s_sm, Dbc[:], bass.AP(dv_d, 0, [[0, 128], [1, D]]))
        dma("sync", s_sm, idb[:], idb_d.ap())
        dma("sync", s_sm, idf[:], idf_d.ap())
        dma("sync", s_sm, onesb[:], onesb_d.ap())
        dma("sync", s_sm, nkrev[:], nkrev_d.ap())
        SM = s_sm.n
        dma("sync", s_on, onesf[:], onesf_d.ap())
        dma("gpsimd", sg, gbc[:], bass.AP(ga_d, 0, [[0, 128], [1, D]]))
        dma("gpsimd", sg, Bbf[:], bw_d.ap())
        m_be = dma("gpsimd", sg, bbc[:], bass.AP(be_d, 0, [[0, 128], [1, D]]))

        emit_x_loads(0)
        stepq.append(attn_pre_gen(0))

        run("vector", lambda e: e.memset(invD[:], 1.0 / D), sv)
        run("vector", lambda e: e.memset(onef[:], 1.0), sv)
        cst[0] = run("vector", lambda e: e.memset(invDFF[:], 1.0 / DFF), sv)
        wait("vector", s_sm, SM)
        m_cbf = run("vector", lambda e: e.tensor_copy(Cbf[:], Csb[:]), sv)
        run("vector", lambda e: e.memset(bxs[:], 0.0), sv)

        wait("scalar", s_sm, SM)
        m_exp = run("scalar", lambda e: e.activation(Esb[:], Asb[:], AF.Exp), sa)
        wait("tensor", s_sm, SM)
        wait("tensor", sa, m_exp)
        m_etr = run("tensor",
                    lambda e: e.transpose(misc1[:, 0:128], Esb[:], idf[:]), stt)
        wait("vector", stt, m_etr)
        m_etre = run("vector", lambda e: e.tensor_copy(EsbT[:], misc1[:, 0:128]),
                     sv)
        wait("scalar", sv, m_etre)
        for dt in range(NDT):
            edst = bass.AP(erow_d, dt * 128, [[D, DS], [1, 128]])
            m_er = dma("scalar", sw, edst, EsbT[dt * 16:(dt + 1) * 16, :])

        wait("tensor", sv, m_cbf)
        for dt in range(NDT):
            m_bctr = run("tensor", lambda e, dt=dt: e.transpose(
                trA[0:16, dt * 128:(dt + 1) * 128], Cbf[:, dt, :], idb[:]), stt)
        wait("vector", stt, m_bctr)
        m_cT = run("vector", lambda e: e.tensor_copy(cT[:], trA[0:16, :]), sv)
        wait("tensor", sv, m_cT)
        wait("tensor", sg, m_be)
        for dt in range(NDT):
            m_btr = run("tensor", lambda e, dt=dt: e.transpose(
                trA[0:128, dt * 16:(dt + 1) * 16],
                Bbf[:, dt * 128:(dt + 1) * 128], idb[0:16, 0:16]), stt)
        wait("vector", stt, m_btr)
        m_bT0 = run("vector", lambda e: e.tensor_copy(
            bT[:].rearrange("p a b -> p (a b)"), trA[0:128, 0:128]), sv)

        pool = [p1a, p1b, p2a, p2b, p2c]
        psem = [sp1[0], sp1[1], sp2[0], sp2[1], sp2[2]]
        buf_rel = {}

        def pool_load(b, src, rearr):
            if b in buf_rel:
                wait("sync", *buf_rel[b])
            dst = (pool[b][:].rearrange("p (a b) -> p a b", a=2) if rearr
                   else pool[b][:])
            return dma("sync", psem[b], dst, src)

        m_absA = []
        for i in range(16):
            src = bass.AP(wup_d, i * 2 * 128 * D,
                          [[D, 128], [128 * D, 2], [1, D]])
            b = i % 5
            mld = pool_load(b, src, True)
            wait("vector", psem[b], mld)
            m_ab = run("vector", lambda e, buf=pool[b], i=i: e.tensor_reduce(
                acw[:, i:i + 1], buf[:], AX, ALU.add,
                apply_absolute_value=True), sv)
            m_absA.append(m_ab)
            buf_rel[b] = (sv, m_ab)
            flush_steps(1)

        wait("vector", sv, m_absA[15])
        m_ru = run("vector", lambda e: e.tensor_reduce(
            acw[:, 32:33], acw[:, 0:16], AX, ALU.add), sv)
        wait("tensor", s_on, 16)
        wait("tensor", sv, m_ru)
        m_ammu = run("tensor", lambda e: e.matmul(
            misc1[:, 0:1], onesf[:], acw[:, 32:33], start=True, stop=True), stt)
        wait("vector", stt, m_ammu)
        m_tau_u = run("vector", lambda e: e.tensor_scalar(
            tau_u[:], misc1[:, 0:1], 0.5 / (DFF * D), None, ALU.mult), sv)

        banks = [trA, trB]
        bank_rel = [0, 0]
        reg_rel = [0, 0]
        upar = [0]
        stg_dma = [0, 0]
        hankf = hank[:].rearrange("p a b -> p (a b)")
        stg_slot = [wstg[:].rearrange("p a b -> p (a b)"),
                    hankf[:, 0:2048]]
        last_t = [0]

        def tern2048(buf, tau, m_tau_w, rpar, dsts):
            wait("vector", stt, reg_rel[rpar])
            wait("vector", sv, last_t[0])
            if m_tau_w:
                wait("vector", sv, m_tau_w)
            m_t1 = run("vector", lambda e, buf=buf, tau=tau: e.tensor_scalar(
                t1bf[:], buf[:], -1.0, tau[:], ALU.mult, ALU.is_gt), sv)
            wait("vector", sv, m_t1)
            m_t = run("vector", lambda e, buf=buf, tau=tau, rpar=rpar:
                      e.scalar_tensor_tensor(
                          wtbf[:, rpar * 2048:(rpar + 1) * 2048], buf[:],
                          tau[:], t1bf[:], ALU.is_gt, ALU.subtract), sv)
            m_trx = None
            for half in range(2):
                par = upar[0] % 2
                upar[0] += 1
                bank = banks[par]
                wait("tensor", sv, m_t)
                wait("tensor", sa, bank_rel[par])
                for k in range(8):
                    off = rpar * 2048 + half * 1024 + k * 128
                    m_trx = run("tensor", lambda e, bank=bank, k=k, off=off:
                                e.transpose(bank[:, k * 128:(k + 1) * 128],
                                            wtbf[:, off:off + 128],
                                            idb[:]), stt)
                wait("scalar", stt, m_trx)
                bank_rel[par] = dsts[half](bank)
            reg_rel[rpar] = m_trx
            last_t[0] = m_t
            return m_t

        def up_evac(fg):
            def f(bank, fg=fg):
                ss = (fg // 2) % 2
                half = fg % 2
                if half == 0:
                    wait("scalar", swp[ss], stg_dma[ss])
                m_ev = run("scalar", lambda e, bank=bank, ss=ss, half=half:
                           e.activation(
                    stg_slot[ss][:, half * 1024:(half + 1) * 1024],
                    bank[:], AF.Copy), sa)
                if half == 1:
                    wait("scalar", sa, m_ev)
                    wdst = bass.AP(wup_scr, (fg // 2) * 128 * 2048,
                                   [[2048, 128], [1, 2048]])
                    stg_dma[ss] = dma("scalar", swp[ss], wdst,
                                      stg_slot[ss])
                return m_ev
            return f

        def dn_evac(j, ft0):
            def f(bank, j=j, ft0=ft0):
                return run("scalar", lambda e, bank=bank: e.activation(
                    WdnT[:, ft0:ft0 + 8, j * 128:(j + 1) * 128],
                    bank[:].rearrange("p (a b) -> p a b", a=8), AF.Copy), sa)
            return f

        m_abs = []
        for i in range(16):
            srcd = bass.AP(wdn_d, (i // 2) * 128 * DFF + (i % 2) * 2048,
                           [[DFF, 128], [1, 2048]])
            bd = i % 2
            mldd = pool_load(bd, srcd, False)
            srcu = bass.AP(wup_d, i * 2 * 128 * D,
                           [[D, 128], [128 * D, 2], [1, D]])
            bu = 2 + i % 3
            mldu = pool_load(bu, srcu, True)
            wait("scalar", psem[bd], mldd)
            if m_abs:
                wait("scalar", sa, m_abs[-1])
            m_ab = run("scalar", lambda e, bufd=pool[bd], i=i: e.activation(
                Ebc[:].rearrange("p a b -> p (a b)"), bufd[:], AF.Abs,
                accum_out=acw[:, 16 + i:17 + i]), sa)
            m_abs.append(m_ab)
            buf_rel[bd] = (sa, m_ab)
            wait("vector", psem[bu], mldu)
            m_t = tern2048(
                pool[bu], tau_u, m_tau_u if i == 0 else 0, i % 2,
                [up_evac(2 * i), up_evac(2 * i + 1)])
            buf_rel[bu] = (sv, m_t)
            flush_steps(1)

        wait("vector", sa, m_abs[15])
        m_rd = run("vector", lambda e: e.tensor_reduce(
            acw[:, 33:34], acw[:, 16:32], AX, ALU.add), sv)
        wait("tensor", sv, m_rd)
        m_ammd = run("tensor", lambda e: e.matmul(
            misc1[:, 1:2], onesf[:], acw[:, 33:34], start=True, stop=True), stt)
        wait("vector", stt, m_ammd)
        m_tau_d = run("vector", lambda e: e.tensor_scalar(
            tau_d[:], misc1[:, 1:2], 0.5 / (DFF * D), None, ALU.mult), sv)

        wait("scalar", sw, m_er)
        m_sl = []
        m_gexp = []
        for i in range(16):
            j, fh = i // 2, i % 2
            src = bass.AP(wdn_d, j * 128 * DFF + fh * 2048,
                          [[DFF, 128], [1, 2048]])
            b = i % 5
            mld = pool_load(b, src, False)
            wait("vector", psem[b], mld)
            m_t = tern2048(
                pool[b], tau_d, m_tau_d if i == 0 else 0, i % 2,
                [dn_evac(j, fh * 16), dn_evac(j, fh * 16 + 8)])
            buf_rel[b] = (sv, m_t)
            if i >= 2:
                na = i - 2
                wait("scalar", sgE[na % 2], m_sl[na])
                if m_gexp:
                    wait("scalar", sa, m_gexp[-1])
                m_ge = run("scalar", lambda e, na=na: e.activation(
                    Msb[:], Ebc[:, na % 2, :], AF.Exp, scale=nkrev[:],
                    accum_out=g_sb[:, na:na + 1]), sa)
                m_gexp.append(m_ge)
            if i < 2:
                wait("scalar", sa, m_abs[15])
            elif m_gexp:
                wait("scalar", sa, m_gexp[-1])
            esrc = bass.AP(erow_d, i * 1024, [[0, 128], [1, 1024]])
            m_sl.append(dma("scalar", sgE[i % 2], Ebc[:, i % 2, :], esrc))
            flush_steps(1)
        for na in (14, 15):
            wait("scalar", sgE[na % 2], m_sl[na])
            wait("scalar", sa, m_gexp[-1])
            m_ge = run("scalar", lambda e, na=na: e.activation(
                Msb[:], Ebc[:, na % 2, :], AF.Exp, scale=nkrev[:],
                accum_out=g_sb[:, na:na + 1]), sa)
            m_gexp.append(m_ge)
        flush_all()

        wait("vector", sa, m_gexp[-1])
        m_gbf = run("vector", lambda e: e.tensor_scalar(
            gbf[:], g_sb[:], 1.0 / D, None, ALU.mult), sv)
        m_z = run("vector", lambda e: e.memset(zrow[:], 0.0), sv)
        wait("tensor", sv, m_gbf)
        wait("tensor", sa, bank_rel[0])
        m_gtr = run("tensor", lambda e: e.transpose(
            trA[0:16, 0:128], gbf[:], idb[:]), stt)
        wait("vector", stt, m_gtr)
        m_gT = run("vector", lambda e: e.tensor_copy(gT[:], trA[0:16, 0:128]),
                   sv)
        wait("scalar", sv, m_gT)
        dma("scalar", sw, bass.AP(gp_d, 128, [[384, DS], [1, 128]]), gT[:])
        dma("scalar", sw, bass.AP(gp_d, 0, [[384, DS], [1, 128]]), zrow[:])
        m_gw = dma("scalar", sw, bass.AP(gp_d, 256, [[384, DS], [1, 128]]),
                   zrow[:])
        wait("scalar", sw, m_gw)
        wait("scalar", swp[1], stg_dma[1])
        for hq in range(4):
            hsrc = bass.AP(gp_d, hq * 4 * 384, [[1, 128], [384, 4], [1, 256]])
            m_hank = dma("scalar", sw, hank[:, hq * 4:(hq + 1) * 4, :], hsrc)
        wait("vector", sw, m_hank)
        m_tall = run("vector", lambda e: e.tensor_copy(
            T_all[:], hank[:, :, ::-1]), sv)

    PRO_V, PRO_A, PRO_T, PRO_W = sv.n, sa.n, stt.n, sw.n
    PRO_SWP = [swp[0].n, swp[1].n]
    PRO_G = sg.n
    for eng in ("sync", "scalar", "vector", "tensor"):
        wait(eng, sv, PRO_V)
        wait(eng, sa, PRO_A)
        wait(eng, stt, PRO_T)
        wait(eng, sw, PRO_W)
        wait(eng, swp[0], PRO_SWP[0])
        wait(eng, swp[1], PRO_SWP[1])
        wait(eng, sg, PRO_G)
        wait(eng, sgE[0], sgE[0].n)
        wait(eng, sgE[1], sgE[1].n)
        wait(eng, s_sm, SM)
        wait(eng, s_on, 16)

    xnT = sb("xnT", [128, NDT, CPS * 128], BF16)
    ln2_sb = sb("ln2", [128, CPS, D], BF16)
    h_sb = sb("h", [128, NFT, CPS * 128], BF16)
    out_sb = sb("out_sb", [128, 2, D], F32)
    tmpA = sb("tmpA", [128, D], F32)
    yf_sb = sb("yf", [128, CPS, D], F32)
    h2_sb = sb("h2", [128, 3, CPS * 128], BF16)
    hm_bf = sb("hm_bf", [128, CPS, DS], BF16)
    hmT = sb("hmT", [16, CPS * 128], BF16)
    qrow = sb("qrow", [1, 512], F32)
    mm1a = pa("mm1a", [128, 512], F32)
    mm1b = pa("mm1b", [128, 512], F32)
    o2set = [[pa("o2Alo", [128, 512], F32), pa("o2Ahi", [128, 512], F32)],
             [pa("o2Blo", [128, 512], F32), pa("o2Bhi", [128, 512], F32)]]
    trA_rel = sv.n

    for sc in range(NSC if STAGE >= 2 else 0):
        row = (sc * CPS) // 16
        j0 = (sc * CPS) % 16
        slot0 = row * 17 + 1 + j0

        m_xnT_ev = []
        m_bxe = []
        for c in range(CPS):
            cc = sc * CPS + c
            wait("tensor", sv, prev_xn[cc])
            wait("tensor", sv, trA_rel)
            m_tr = None
            for dt in range(NDT):
                m_tr = run("tensor", lambda e, dt=dt, c=c: e.transpose(
                    trA[:, dt * 128:(dt + 1) * 128],
                    xnbf[:, c, dt * 128:(dt + 1) * 128], idb[:]), stt)
            tr_mark[cc] = m_tr
            wait("vector", stt, m_tr)
            if c == 0:
                wait("vector", stt, prev_mm1_done)
            m_ev = run("vector", lambda e, c=c: e.tensor_copy(
                xnT[:, :, c * 128:(c + 1) * 128],
                trA[:].rearrange("p (a b) -> p a b", a=NDT)), sv)
            trA_rel = m_ev
            m_xnT_ev.append(m_ev)
            wait("tensor", sv, m_ev)
            if c == 0 and misc1_rel[0] is not None:
                wait("tensor", *misc1_rel)
            m_bx = None
            for dt in range(NDT):
                m_bx = run("tensor", lambda e, dt=dt, c=c: e.matmul(
                    misc1[:, c * 16:(c + 1) * 16],
                    xnT[:, dt, c * 128:(c + 1) * 128],
                    bT[:, dt, :], start=(dt == 0), stop=(dt == NDT - 1)), stt)
            wait("vector", stt, m_bx)
            slot = row * 17 + 1 + (cc % 16)
            m_bxev = run("vector", lambda e, c=c, cc=cc, slot=slot:
                         e.tensor_scalar(
                bxs[:, slot, :], misc1[:, c * 16:(c + 1) * 16],
                col(AC, cc), None, ALU.mult), sv)
            m_bxe.append(m_bxev)

        wait("tensor", sv, m_bxe[-1])
        m_cv = None
        for n in range(DS):
            run("tensor", lambda e, n=n: e.matmul(
                misc1[:, 256 + n * CPS:256 + (n + 1) * CPS],
                T_all[:, n, 0:128], bxs[:, slot0:slot0 + CPS, n],
                start=True, stop=False), None)
            m_cv = run("tensor", lambda e, n=n: e.matmul(
                misc1[:, 256 + n * CPS:256 + (n + 1) * CPS],
                T_all[:, n, 128:256], bxs[:, slot0 - 1:slot0 - 1 + CPS, n],
                start=False, stop=True), stt)
        wait("vector", stt, m_cv)
        m_hm = run("vector", lambda e: e.tensor_copy(
            hm_bf[:], misc1[:, 256:256 + DS * CPS].rearrange(
                "p (n c) -> p c n", n=DS)), sv)
        misc1_rel = (sv, m_hm)

        m_hte = 0
        ln2_args = []
        for c in range(CPS):
            cc = sc * CPS + c
            xslot = cc % XSLOTS
            oset = o2set[cc % 2]
            yfb = yf_sb[:, c, :]
            wait("tensor", sv, m_hm if c == 0 else m_hte)
            wait("tensor", sv, trA_rel)
            m_ht = run("tensor", lambda e, c=c: e.transpose(
                trA[0:16, c * 128:(c + 1) * 128], hm_bf[:, c, :], idb[:]), stt)
            wait("vector", stt, m_ht)
            m_hte = run("vector", lambda e, c=c: e.tensor_copy(
                hmT[:, c * 128:(c + 1) * 128],
                trA[0:16, c * 128:(c + 1) * 128]), sv)
            trA_rel = m_hte
            m_dxa = 0
            for hf in range(2):
                wait("tensor", sv, m_hte if hf == 0 else m_dxa)
                if hf == 0:
                    wait("tensor", sv, o2_rel[cc % 2])
                m_y = run("tensor", lambda e, c=c, hf=hf, oset=oset: e.matmul(
                    oset[hf][:], hmT[:, c * 128:(c + 1) * 128],
                    cT[:, hf * 512:(hf + 1) * 512], start=True, stop=True), stt)
                if c == 0 and hf == 0:
                    wait("vector", sv, last_asm_mark)
                dxm = run("vector", lambda e, xslot=xslot, hf=hf: e.tensor_mul(
                    tmpA[:, hf * 512:(hf + 1) * 512],
                    x_at[:, xslot, hf * 512:(hf + 1) * 512],
                    Dbc[:, hf * 512:(hf + 1) * 512]), sv)
                wait("vector", stt, m_y)
                wait("vector", sv, dxm)
                if hf == 0 and cc >= CPS:
                    wait("vector", sv, m_l2m.get(cc - CPS, 0))
                m_dxa = run("vector", lambda e, hf=hf, oset=oset, yfb=yfb:
                            e.tensor_add(
                    yfb[:, hf * 512:(hf + 1) * 512], oset[hf][:],
                    tmpA[:, hf * 512:(hf + 1) * 512]), sv)
            prev_dx[cc] = m_dxa
            ln2_args.append((m_dxa, yfb))

        stepq.append(ln2_gen(sc, ln2_args))
        if sc + 1 < NSC:
            emit_x_loads(sc + 1, range(2))
            stepq.append(attn_pre_gen(sc + 1, range(2)))

        if STAGE < 2:
            continue
        mm1banks = [mm1a, mm1b]
        m_h2 = [None] * NFT
        m_qmm = [0] * NFT
        wait("tensor", sv, m_xnT_ev[-1])
        if sc == 1:
            wait("tensor", *misc1_rel)

        def issue_wup(sup):
            slot = sup % 2
            wait("scalar", stt, wup_rel[slot])
            wsrc = bass.AP(wup_scr, sup * 128 * 2048,
                           [[2048, 128], [1, 2048]])
            wup_ld[sup] = dma("scalar", swp[slot],
                              wupb[:, slot, :, :, :].rearrange(
                                  "p a b c -> p (a b c)"), wsrc)

        def emit_qmm(fg):
            wait("tensor", sv, m_h2[fg])
            if fg == 0:
                wait("tensor", *misc1_rel)
            m_qmm[fg] = run("tensor", lambda e, fg=fg: e.matmul(
                misc1[0:1, 0:512], onesb[:], h2_sb[:, fg % 3, :],
                start=(fg == 0), stop=(fg == NFT - 1),
                skip_group_check=True), stt)

        issue_wup(0)
        issue_wup(1)
        for fg in range(NFT):
            sup, fi = fg // 2, fg % 2
            bank = mm1banks[fg % 2]
            wait("tensor", sa, mm1_rel[fg % 2])
            if fi == 0:
                wait("tensor", swp[sup % 2], wup_ld[sup])
            m_m1 = None
            for dt in range(NDT):
                m_m1 = run("tensor", lambda e, bank=bank, dt=dt, sup=sup,
                           fi=fi: e.matmul(
                    bank[:, 0:CPS * 128], wupb[:, sup % 2, fi, dt, :],
                    xnT[:, dt, :], start=(dt == 0), stop=(dt == NDT - 1)), stt)
            if fi == 1:
                wup_rel[sup % 2] = m_m1
            if fg == NFT - 1:
                prev_mm1_done = m_m1
            if fg >= 2:
                emit_qmm(fg - 2)
            wait("scalar", stt, m_m1)
            if fi == 1 and sup + 2 < NFT // 2:
                issue_wup(sup + 2)
            if sc > 0 and fg < 2:
                wait("scalar", stt, prev_mm2_done)
            m_g1 = run("scalar", lambda e, bank=bank, fg=fg: e.activation(
                h_sb[:, fg, :], bank[:, 0:CPS * 128], AF.Gelu), sa)
            mm1_rel[fg % 2] = m_g1
            wait("vector", sa, m_g1)
            if fg >= 3:
                wait("vector", stt, m_qmm[fg - 3])
            elif sc > 0:
                wait("vector", stt, prev_q_last)
            m_h2[fg] = run("vector", lambda e, fg=fg: e.tensor_mul(
                h2_sb[:, fg % 3, :], h_sb[:, fg, :], h_sb[:, fg, :]), sv)
            flush_steps(2)
        emit_qmm(NFT - 2)
        emit_qmm(NFT - 1)
        prev_q_last = m_qmm[NFT - 1]
        flush_all()

        QS, SQ = 18, 19
        m_sce = 0
        wait("vector", stt, m_qmm[NFT - 1])
        m_qrow = run("vector", lambda e: e.tensor_copy(
            qrow[:], misc1[0:1, 0:512]), sv)
        wait("tensor", sv, m_qrow)
        sqm = 0
        for c in range(CPS):
            m_qtr = run("tensor", lambda e, c=c: e.transpose(
                misc1[:, 320 + c:321 + c], qrow[0:1, c * 128:(c + 1) * 128],
                onef[:]), stt)
            wait("scalar", stt, m_qtr)
            sqm = run("scalar", lambda e, c=c: e.activation(
                stc[:, SQ + 24 * c:SQ + 24 * c + 1],
                misc1[:, 320 + c:321 + c], AF.Sqrt,
                scale=invDFF[:]), sa)
            wait("vector", sa, sqm)
            d1 = run("vector", lambda e, c=c: e.tensor_scalar(
                stc[:, QS + 24 * c:QS + 24 * c + 1],
                stc[:, SQ + 24 * c:SQ + 24 * c + 1], EPS2, None, ALU.add), sv)
            wait("vector", sv, d1)
            m_sce = run("vector", lambda e, c=c: e.reciprocal(
                scol[:, c:c + 1], stc[:, QS + 24 * c:QS + 24 * c + 1]), sv)
        misc1_rel = (sa, sqm)

        for c in range(CPS):
            cc = sc * CPS + c
            r, jj = cc // 16, cc % 16
            xslot = cc % XSLOTS
            ab = cc % 2
            oset = o2set[cc % 2]
            m_m2l = m_m2h = None
            wait("tensor", sv, o2_rel[cc % 2])
            wait("tensor", sv, prev_dx[cc])
            for fg in range(NFT):
                m_m2l = run("tensor", lambda e, c=c, fg=fg, oset=oset: e.matmul(
                    oset[0][:], h_sb[:, fg, c * 128:(c + 1) * 128],
                    WdnT[:, fg, 0:512],
                    start=(fg == 0), stop=(fg == NFT - 1),
                    skip_group_check=True), stt)
                m_m2h = run("tensor", lambda e, c=c, fg=fg, oset=oset: e.matmul(
                    oset[1][:], h_sb[:, fg, c * 128:(c + 1) * 128],
                    WdnT[:, fg, 512:1024],
                    start=(fg == 0), stop=(fg == NFT - 1),
                    skip_group_check=True), stt)
            if c == CPS - 1:
                prev_mm2_done = m_m2h
            if sc == 0 and c == 0:
                wait("vector", sg, m_be)
            wait("vector", sv, last_asm_mark)
            wait("vector", sv, m_l2m[cc])
            p1 = run("vector", lambda e, c=c: e.tensor_mul(
                tmpA[:], ln2_sb[:, c, :], gbc[:]), sv)
            wait("vector", sv, p1)
            p2 = run("vector", lambda e: e.tensor_add(tmpA[:], tmpA[:], bbc[:]),
                sv)
            wait("vector", sv, p2)
            p3 = run("vector", lambda e, xslot=xslot: e.tensor_add(
                tmpA[:], tmpA[:], x_at[:, xslot, :]), sv)
            wait("vector", sv, p3)
            if cc >= 2:
                wait("vector", soo[ab], m_out.get(cc - 2, 0))
            wait("vector", stt, m_m2h)
            run("vector", lambda e, c=c, ab=ab, oset=oset: e.scalar_tensor_tensor(
                out_sb[:, ab, 0:512], oset[0][:], scol[:, c:c + 1],
                tmpA[:, 0:512], ALU.mult, ALU.add), None)
            m_asm = run("vector", lambda e, c=c, ab=ab, oset=oset: e.scalar_tensor_tensor(
                out_sb[:, ab, 512:1024], oset[1][:], scol[:, c:c + 1],
                tmpA[:, 512:1024], ALU.mult, ALU.add), sv)
            o2_rel[cc % 2] = m_asm
            prev_asm[cc] = m_asm
            last_asm_mark = m_asm
            wait("sync", sv, m_asm)
            ydst = bass.AP(y_d, (r * SEQ + jj * 128) * D, [[D, 128], [1, D]])
            m_o = dma("sync", soo[ab], ydst, out_sb[:, ab, :])
            m_out[cc] = m_o

        if sc + 1 < NSC:
            emit_x_loads(sc + 1, [2, 3])
            stepq.append(attn_pre_gen(sc + 1, [2, 3]))
            flush_all()

    if STAGE >= 3:
        wait("sync", soo[0], soo[0].n)
        wait("sync", soo[1], soo[1].n)
    else:
        wait("sync", sv, sv.n)
        wait("sync", sa, sa.n)
        wait("sync", stt, stt.n)

    with nc.Block() as block:
        @block.sync
        def _(e):
            for f in ops["sync"]:
                f(e)

        @block.scalar
        def _(e):
            for f in ops["scalar"]:
                f(e)

        @block.vector
        def _(e):
            for f in ops["vector"]:
                f(e)

        @block.tensor
        def _(e):
            for f in ops["tensor"]:
                f(e)

        @block.gpsimd
        def _(e):
            for f in ops["gpsimd"]:
                f(e)

    return nc


_NC = None


def kernel(L0_attn_x, L0_attn_A_log, L0_attn_B_w, L0_attn_C_w, L0_attn_D,
           L0_attn_gamma, L0_attn_beta, L0_ffn_w_up, L0_ffn_w_down):
    global _NC
    f32 = np.float32
    x = np.ascontiguousarray(np.asarray(L0_attn_x, f32))
    wup = np.ascontiguousarray(np.asarray(L0_ffn_w_up, f32))
    wdn = np.ascontiguousarray(np.asarray(L0_ffn_w_down, f32))
    alog = np.ascontiguousarray(np.asarray(L0_attn_A_log, f32))
    bw = np.ascontiguousarray(np.asarray(L0_attn_B_w, f32))
    cw = np.ascontiguousarray(np.asarray(L0_attn_C_w, f32))
    dv = np.ascontiguousarray(np.asarray(L0_attn_D, f32)).reshape(1, D)
    ga = np.ascontiguousarray(np.asarray(L0_attn_gamma, f32)).reshape(1, D)
    be = np.ascontiguousarray(np.asarray(L0_attn_beta, f32)).reshape(1, D)

    idb = np.eye(128, dtype=ml_dtypes.bfloat16)
    idf = np.eye(128, dtype=f32)
    onesb = np.ones((128, 1), dtype=ml_dtypes.bfloat16)
    onesf = np.ones((128, 128), dtype=f32)
    nkrev = (np.arange(128, dtype=f32) - 127.0).reshape(128, 1)

    if _NC is None:
        _NC = build_nc()

    in_maps = [{
        "x": x[2 * core:2 * core + 2],
        "wup": wup, "wdn": wdn, "alog": alog, "bw": bw, "cw": cw,
        "dv": dv, "ga": ga, "be": be,
        "idb": idb, "idf": idf, "onesb": onesb, "onesf": onesf,
        "nkrev": nkrev,
    } for core in range(8)]
    res = run_bass_kernel_spmd(_NC, in_maps, core_ids=list(range(8)))
    return np.concatenate([r["y"] for r in res.results], axis=0)
